# version 15
# baseline (speedup 1.0000x reference)
import numpy as np
import ml_dtypes
from contextlib import ExitStack

import concourse.bass as bass
import concourse.mybir as mybir
from concourse.bass_utils import run_bass_kernel_spmd

F32 = mybir.dt.float32
BF16 = mybir.dt.bfloat16
AF = mybir.ActivationFunctionType
ALU = mybir.AluOpType
AX = mybir.AxisListType

S = 4096
D = 1024
H = 16
HD = 64
ROPE = 32
NB = 8
NT = 32
EPS = 1e-5
BF = ml_dtypes.bfloat16


class Buf:
    __slots__ = ("w", "r")

    def __init__(self):
        self.w = None
        self.r = {}


class Sched:
    ENGNAME = {"t": "tensor", "v": "vector", "s": "scalar", "p": "gpsimd", "q": "sync"}

    NL = 8

    def __init__(self, nc, stack):
        self.nc = nc
        self.semkeys = (["t", "v", "s", "p"]
                        + ["q%d" % i for i in range(self.NL)]
                        + ["p%d" % i for i in range(self.NL)]
                        + ["s%d" % i for i in range(4)])
        self.sems = {k: stack.enter_context(nc.semaphore("sem_" + k)) for k in self.semkeys}
        self.cnt = {k: 0 for k in self.semkeys}
        self.waited = {e: {k: 0 for k in self.semkeys} for e in "tvspq"}
        self.thunks = {e: [] for e in "tvspq"}
        self.dmaidx = {"q": 0, "p": 0, "s": 0}

    def _wait(self, e, semkey, val):
        if val <= 0:
            return
        if e == "t" and semkey == "t":
            return
        if self.waited[e][semkey] >= val:
            return
        self.waited[e][semkey] = val
        sem = self.sems[semkey]
        self.thunks[e].append(lambda eng, s=sem, v=val: eng.wait_ge(s, v))

    def op(self, e, emit, reads=(), writes=(), dma=False):
        for b in reads:
            if b.w is not None:
                self._wait(e, b.w[0], b.w[1])
        for b in writes:
            if b.w is not None:
                self._wait(e, b.w[0], b.w[1])
            for sk, v in b.r.items():
                self._wait(e, sk, v)
        if dma:
            nl = 4 if e == "s" else self.NL
            lane = self.dmaidx[e] % nl
            self.dmaidx[e] += 1
            semkey = "%s%d" % (e, lane)
            inc = 16
            self._wait(e, semkey, self.cnt[semkey])
        else:
            semkey = e
            inc = 1
        self.cnt[semkey] += inc
        tag = (semkey, self.cnt[semkey])
        sem = self.sems[semkey]
        self.thunks[e].append(lambda eng, f=emit, s=sem, i=inc: f(eng).then_inc(s, i))
        for b in writes:
            b.w = tag
            b.r = {}
        for b in reads:
            if b.r.get(tag[0], 0) < tag[1]:
                b.r[tag[0]] = tag[1]
        return tag

    def drain_dmas(self):
        for i in range(self.NL):
            self._wait("q", "q%d" % i, self.cnt["q%d" % i])
            self._wait("q", "p%d" % i, self.cnt["p%d" % i])
        for i in range(4):
            self._wait("q", "s%d" % i, self.cnt["s%d" % i])

    def emit_block(self):
        with self.nc.Block() as blk:
            for e in "tvspq":
                if not self.thunks[e]:
                    continue

                def runner(eng, ths=self.thunks[e]):
                    for th in ths:
                        th(eng)

                getattr(blk, self.ENGNAME[e])(runner)
        self.thunks = {e: [] for e in "tvspq"}


def build_nc():
    nc = bass.Bass("TRN2", target_bir_lowering=False)

    x_d = nc.dram_tensor("x", [S, D], F32, kind="ExternalInput").ap()
    wq_d = nc.dram_tensor("wqkvT", [D, 3 * D], F32, kind="ExternalInput").ap()
    wo_d = nc.dram_tensor("woutT", [D, D], F32, kind="ExternalInput").ap()
    cw1_d = nc.dram_tensor("cnw1", [D, D], F32, kind="ExternalInput").ap()
    cw2_d = nc.dram_tensor("cnw2", [D, 512], F32, kind="ExternalInput").ap()
    cw3_d = nc.dram_tensor("cnw3", [512, 256], F32, kind="ExternalInput").ap()
    cw4_d = nc.dram_tensor("cnw4", [128, 2], F32, kind="ExternalInput").ap()
    gw1_d = nc.dram_tensor("gcw1", [D, 512], F32, kind="ExternalInput").ap()
    gw2_d = nc.dram_tensor("gcw2", [512, 256], F32, kind="ExternalInput").ap()
    gw3_d = nc.dram_tensor("gcw3r", [256, 128], F32, kind="ExternalInput").ap()
    b1_d = nc.dram_tensor("b1", [128, 8], F32, kind="ExternalInput").ap()
    b2_d = nc.dram_tensor("b2", [128, 4], F32, kind="ExternalInput").ap()
    b3_d = nc.dram_tensor("b3", [128, 2], F32, kind="ExternalInput").ap()
    b4_d = nc.dram_tensor("b4", [128, 1], F32, kind="ExternalInput").ap()
    g1_d = nc.dram_tensor("g1", [128, 4], F32, kind="ExternalInput").ap()
    g2_d = nc.dram_tensor("g2", [128, 2], F32, kind="ExternalInput").ap()
    g3_d = nc.dram_tensor("g3", [128, 1], F32, kind="ExternalInput").ap()
    ar_d = nc.dram_tensor("ar", [128, 1], F32, kind="ExternalInput").ap()
    idb_d = nc.dram_tensor("identb", [128, 128], BF16, kind="ExternalInput").ap()
    idf_d = nc.dram_tensor("identf", [128, 128], F32, kind="ExternalInput").ap()
    tri_d = nc.dram_tensor("tri", [128, 128], BF16, kind="ExternalInput").ap()
    ones_d = nc.dram_tensor("onescol", [128, 1], BF16, kind="ExternalInput").ap()
    cos_d = nc.dram_tensor("cosf", [128, S], F32, kind="ExternalInput").ap()
    sin_d = nc.dram_tensor("sinf", [128, S], F32, kind="ExternalInput").ap()
    r128_d = nc.dram_tensor("r128t", [128, 128], BF16, kind="ExternalInput").ap()

    qf_d = nc.dram_tensor("qf", [D, S], BF16, kind="Internal").ap()
    kf_d = nc.dram_tensor("kf", [D, S], BF16, kind="Internal").ap()
    vt_d = nc.dram_tensor("vt", [S, D], BF16, kind="Internal").ap()
    x1_d = nc.dram_tensor("x1", [S, D], F32, kind="Internal").ap()
    out_d = nc.dram_tensor("out", [S, D], F32, kind="ExternalOutput").ap()

    root = ExitStack()
    sch = Sched(nc, root)

    def sb(name, shape, dtype, stk):
        t = stk.enter_context(nc.sbuf_tensor("sb_" + name, shape, dtype))
        return t, Buf()

    def ps(name, shape, dtype, stk):
        t = stk.enter_context(nc.psum_tensor("ps_" + name, shape, dtype))
        return t, Buf()

    qf_b, kf_b, vt_b, x1_b, out_b = Buf(), Buf(), Buf(), Buf(), Buf()
    xd_b = Buf()

    identb, identb_b = sb("identb", [128, 128], BF16, root)
    identf, identf_b = sb("identf", [128, 128], F32, root)
    tri, tri_b = sb("tri", [128, 128], BF16, root)
    onescol, onescol_b = sb("onescol", [128, 1], BF16, root)
    b1s, b1s_b = sb("b1s", [128, 8], F32, root)
    b2s, b2s_b = sb("b2s", [128, 4], F32, root)
    b3s, b3s_b = sb("b3s", [128, 2], F32, root)
    b4s, b4s_b = sb("b4s", [128, 1], F32, root)
    g1s, g1s_b = sb("g1s", [128, 4], F32, root)
    g2s, g2s_b = sb("g2s", [128, 2], F32, root)
    g3s, g3s_b = sb("g3s", [128, 1], F32, root)
    ars, ars_b = sb("ars", [128, 1], F32, root)
    epsc, epsc_b = sb("epsc", [128, 1], F32, root)
    gcf, gcf_b = sb("gcf", [128, 8], F32, root)

    def load(e, dst_ap, src_ap, rbufs, wbufs):
        sch.op(e, lambda eng, d=dst_ap, s=src_ap: eng.dma_start(d, s),
               reads=rbufs, writes=wbufs, dma=True)

    for dst, dstb, src in [
        (identb, identb_b, idb_d), (identf, identf_b, idf_d), (tri, tri_b, tri_d),
        (onescol, onescol_b, ones_d), (b1s, b1s_b, b1_d), (b2s, b2s_b, b2_d),
        (b3s, b3s_b, b3_d), (b4s, b4s_b, b4_d), (g1s, g1s_b, g1_d),
        (g2s, g2s_b, g2_d), (g3s, g3s_b, g3_d), (ars, ars_b, ar_d),
    ]:
        load("q", dst[:], src[:], [], [dstb])
    sch.op("v", lambda eng: eng.memset(epsc[:], EPS), reads=[], writes=[epsc_b])

    p1 = ExitStack()
    wq, wq_b = sb("wq", [128, 8, 3 * D], BF16, p1)
    cosf, cosf_b = sb("cosf", [128, S], F32, p1)
    sinf, sinf_b = sb("sinf", [128, S], F32, p1)
    r128, r128_b = sb("r128", [128, 128], BF16, p1)
    load("q", r128[:], r128_d[:], [], [r128_b])
    load("q", cosf[:], cos_d[:], [], [cosf_b])
    load("q", sinf[:], sin_d[:], [], [sinf_b])
    wq_src = wq_d.rearrange("(c p) f -> p c f", p=128)
    load("p", wq[:, 0:4, :], wq_src[:, 0:4, :], [], [wq_b])
    load("p", wq[:, 4:8, :], wq_src[:, 4:8, :], [], [wq_b])

    xt = [sb("xt%d" % i, [128, D], F32, p1) for i in range(2)]
    xtb = [sb("xtb%d" % i, [128, D], BF16, p1) for i in range(2)]
    xf = [sb("xf%d" % i, [128, 8, 512], BF16, p1) for i in range(2)]
    qkraw, qkraw_b = sb("qkraw", [128, 16, 512], BF16, p1)
    qkraw_bufs = [Buf() for _ in range(16)]
    qkro = [sb("qkro%d" % i, [128, 16, 512], BF16, p1) for i in range(2)]
    vraw = [sb("vraw%d" % i, [128, 512], BF16, p1) for i in range(2)]
    vsb, vsb_b = sb("vsb", [128, 8, 4, 128], BF16, p1)
    tsin = [sb("tsin%d" % i, [128, 512], F32, p1) for i in range(2)]
    tcos = [sb("tcos%d" % i, [128, 512], F32, p1) for i in range(2)]

    trp = [ps("trp%d" % i, [128, 1024], BF16, p1) for i in range(2)]
    qps = [ps("qps%d" % i, [128, 512], F32, p1) for i in range(2)]
    rp = [ps("rp%d" % i, [128, 512], F32, p1) for i in range(2)]
    vtp, vtp_b = ps("vtp", [128, 512], BF16, p1)

    for b in range(NB):
        xfb, xfb_buf = xf[b % 2]
        qkro_t, qkro_buf = qkro[b % 2]
        for tau in range(4):
            g = 4 * b + tau
            xt_t, xt_buf = xt[g % 2]
            xtb_t, xtb_buf = xtb[g % 2]
            trp_t, trp_buf = trp[g % 2]
            load("q", xt_t[:], x_d[128 * g:128 * (g + 1), :], [xd_b], [xt_buf])
            sch.op("s", lambda eng, o=xtb_t, i=xt_t: eng.copy(o[:], i[:]),
                   reads=[xt_buf], writes=[xtb_buf])
            for c in range(8):
                sch.op("t", lambda eng, o=trp_t, i=xtb_t, c=c:
                       eng.transpose(o[:, 128 * c:128 * (c + 1)], i[:, 128 * c:128 * (c + 1)], identb[:]),
                       reads=[xtb_buf, identb_b], writes=[trp_buf])
            dst = xfb[:].rearrange("p c (t4 e) -> p c t4 e", t4=4)[:, :, tau, :]
            src = trp_t[:].rearrange("p (c e) -> p c e", c=8)
            sch.op("v", lambda eng, d=dst, s=src: eng.tensor_copy(d, s),
                   reads=[trp_buf], writes=[xfb_buf])
        for fo in range(24):
            qp_t, qp_buf = qps[fo % 2]
            for p in range(8):
                sch.op("t", lambda eng, o=qp_t, fo=fo, p=p, x=xfb:
                       eng.matmul(o[:], wq[:, p, 128 * fo:128 * (fo + 1)], x[:, p, :],
                                  start=(p == 0), stop=(p == 7), skip_group_check=True),
                       reads=[wq_b, xfb_buf], writes=[qp_buf])
            if fo < 16:
                c = fo
                sch.op("v", lambda eng, c=c, i=qp_t: eng.tensor_copy(qkraw[:, c, :], i[:]),
                       reads=[qp_buf], writes=[qkraw_bufs[c]])
            else:
                c = fo - 16
                vr_t, vr_buf = vraw[c % 2]
                sch.op("v", lambda eng, o=vr_t, i=qp_t: eng.tensor_copy(o[:], i[:]),
                       reads=[qp_buf], writes=[vr_buf])
                for tau in range(4):
                    sch.op("t", lambda eng, i=vr_t, tau=tau:
                           eng.transpose(vtp[:, 128 * tau:128 * (tau + 1)], i[:, 128 * tau:128 * (tau + 1)], identb[:]),
                           reads=[vr_buf, identb_b], writes=[vtp_b])
                sch.op("v", lambda eng, c=c: eng.tensor_copy(
                    vsb[:, c, :, :], vtp[:].rearrange("p (t4 e) -> p t4 e", t4=4)),
                    reads=[vtp_b], writes=[vsb_b])
        for c in range(16):
            rp_t, rp_buf = rp[c % 2]
            ts_t, ts_buf = tsin[c % 2]
            tc_t, tc_buf = tcos[c % 2]
            sch.op("t", lambda eng, o=rp_t, c=c:
                   eng.matmul(o[:], r128[:], qkraw[:, c, :], start=True, stop=True, skip_group_check=True),
                   reads=[r128_b, qkraw_bufs[c]], writes=[rp_buf])
            sch.op("v", lambda eng, o=ts_t, i=rp_t, b=b:
                   eng.tensor_tensor(o[:], i[:], sinf[:, 512 * b:512 * (b + 1)], ALU.mult),
                   reads=[rp_buf, sinf_b], writes=[ts_buf])
            sch.op("p", lambda eng, o=tc_t, c=c, b=b:
                   eng.tensor_tensor(o[:], qkraw[:, c, :], cosf[:, 512 * b:512 * (b + 1)], ALU.mult),
                   reads=[qkraw_bufs[c], cosf_b], writes=[tc_buf])
            sch.op("p", lambda eng, o=qkro_t, c=c, i0=tc_t, i1=ts_t:
                   eng.tensor_tensor(o[:, c, :], i0[:], i1[:], ALU.add),
                   reads=[tc_buf, ts_buf], writes=[qkro_buf])
        qdst = qf_d.rearrange("(c p) s -> p c s", p=128)[:, :, 512 * b:512 * (b + 1)]
        kdst = kf_d.rearrange("(c p) s -> p c s", p=128)[:, :, 512 * b:512 * (b + 1)]
        vdst = vt_d.rearrange("(B t4 p) (c e) -> B p c t4 e", t4=4, p=128, e=128)[b]
        sch.op("p", lambda eng, d=qdst, s=qkro_t: eng.dma_start(d, s[:, 0:8, :]),
               reads=[qkro_buf], writes=[qf_b], dma=True)
        sch.op("p", lambda eng, d=kdst, s=qkro_t: eng.dma_start(d, s[:, 8:16, :]),
               reads=[qkro_buf], writes=[kf_b], dma=True)
        sch.op("p", lambda eng, d=vdst: eng.dma_start(d, vsb[:]),
               reads=[vsb_b], writes=[vt_b], dma=True)

    sch.drain_dmas()
    sch.emit_block()
    p1.close()

    p23 = ExitStack()
    aot, aot_b = sb("aot", [128, NT, D], BF16, p23)
    wo, wo_b = sb("wo", [128, 8, D], BF16, p23)
    load("p", wo[:], wo_d.rearrange("(c p) f -> p c f", p=128), [], [wo_b])

    p2 = ExitStack()
    kst = [sb("kst%d" % i, [128, S], BF16, p2) for i in range(4)]
    qst = [sb("qst%d" % i, [128, S], BF16, p2) for i in range(4)]
    vst = [sb("vst%d" % i, [128, NT, 130], BF16, p2) for i in range(2)]
    Ppair = [[sb("Pp%d_%d" % (h2, i), [128, 1024], BF16, p2) for i in range(3)] for h2 in range(2)]
    rcp = [sb("rcp%d" % i, [128, 4], F32, p2) for i in range(4)]
    pairps = [ps("pair%d" % i, [128, 1024], F32, p2) for i in range(2)]
    pvps = [[ps("pv%d_%d" % (h2, i), [128, 260], F32, p2) for i in range(2)] for h2 in range(2)]

    for i in range(4):
        t, bfk = kst[i]
        sch.op("v", lambda eng, t=t: eng.memset(t[64:128, :], 0.0), reads=[], writes=[bfk])
        t, bfq2 = qst[i]
        sch.op("v", lambda eng, t=t: eng.memset(t[64:128, :], 0.0), reads=[], writes=[bfq2])
    for i in range(2):
        t, bfv = vst[i]
        sch.op("v", lambda eng, t=t: eng.memset(t[:].rearrange("p i (h2 e) -> p i h2 e", e=65)[:, :, :, 64], 1.0),
               reads=[], writes=[bfv])

    vt_src = vt_d.rearrange("(i p) e -> p i e", p=128)

    def emit_pair_loads(u):
        ub = u % 2
        ks = [kst[2 * ub], kst[2 * ub + 1]]
        qs = [qst[2 * ub], qst[2 * ub + 1]]
        vst_t, vst_buf = vst[ub]
        for h2 in (0, 1):
            h = 2 * u + h2
            load("q", ks[h2][0][0:64, :], kf_d[64 * h:64 * (h + 1), :], [kf_b], [ks[h2][1]])
            load("q", qs[h2][0][0:64, :], qf_d[64 * h:64 * (h + 1), :], [qf_b], [qs[h2][1]])
        vv = vst_t[:].rearrange("p i (h2 e) -> p i h2 e", e=65)
        load("q", vv[:, :, 0, 0:64], vt_src[:, :, 128 * u:128 * u + 64], [vt_b], [vst_buf])
        load("q", vv[:, :, 1, 0:64], vt_src[:, :, 128 * u + 64:128 * u + 128], [vt_b], [vst_buf])

    emit_pair_loads(0)
    for u in range(H // 2):
        ub = u % 2
        ks = [kst[2 * ub], kst[2 * ub + 1]]
        qs = [qst[2 * ub], qst[2 * ub + 1]]
        vst_t, vst_buf = vst[ub]
        for sbk in range(NB):
            if sbk == 1 and u + 1 < H // 2:
                emit_pair_loads(u + 1)
            nI = 4 * sbk + 4
            npair = nI // 2
            pvs = [pvps[0][sbk % 2], pvps[1][sbk % 2]]
            first = [True, True]

            def emit_pv_pair(pp, h2, sbk=sbk, nI=nI, pvs=pvs, first=first,
                             vst_t=vst_t, vst_buf=vst_buf):
                Pp_t, Pp_buf = Ppair[h2][pp % 3]
                pv_t, pv_buf = pvs[h2]
                for half in (0, 1):
                    i = 2 * pp + half
                    for jj in range(4):
                        if i > 4 * sbk + jj:
                            continue
                        st0 = first[h2]
                        first[h2] = False
                        last = (i == nI - 1) and (jj == 3)
                        sch.op("t", lambda eng, o=pv_t, P=Pp_t, half=half, jj=jj, i=i, h2=h2,
                               st0=st0, last=last, v=vst_t:
                               eng.matmul(o[:, 65 * jj:65 * jj + 65],
                                          P[:, 512 * half + 128 * jj:512 * half + 128 * (jj + 1)],
                                          v[:, i, 65 * h2:65 * h2 + 65],
                                          start=st0, stop=last, skip_group_check=True),
                               reads=[Pp_buf, vst_buf], writes=[pv_buf])

            for pp in range(npair):
                for h2 in (0, 1):
                    pa_t, pa_buf = pairps[h2]
                    Pp_t, Pp_buf = Ppair[h2][pp % 3]
                    kst_t, kst_buf = ks[h2]
                    qst_t, qst_buf = qs[h2]
                    for half in (0, 1):
                        i = 2 * pp + half
                        sch.op("t", lambda eng, o=pa_t, half=half, i=i, k=kst_t, q=qst_t, sbk=sbk:
                               eng.matmul(o[:, 512 * half:512 * (half + 1)], k[:, 128 * i:128 * (i + 1)],
                                          q[:, 512 * sbk:512 * (sbk + 1)],
                                          start=True, stop=True, skip_group_check=True),
                               reads=[kst_buf, qst_buf], writes=[pa_buf])
                    r0 = 2 * pp - 4 * sbk
                    if r0 >= 0:
                        for half in (0, 1):
                            r = r0 + half
                            c0 = 512 * half + 128 * r
                            sch.op("s", lambda eng, o=Pp_t, i2=pa_t, c0=c0, c1=512 * (half + 1):
                                   eng.activation(o[:, c0:c1], i2[:, c0:c1], AF.Exp, scale=0.125),
                                   reads=[pa_buf], writes=[Pp_buf])
                            sch.op("v", lambda eng, P=Pp_t, c0=c0:
                                   eng.tensor_tensor(P[:, c0:c0 + 128], P[:, c0:c0 + 128], tri[:], ALU.mult),
                                   reads=[Pp_buf, tri_b], writes=[Pp_buf])
                    else:
                        sch.op("s", lambda eng, o=Pp_t, i2=pa_t:
                               eng.activation(o[:], i2[:], AF.Exp, scale=0.125),
                               reads=[pa_buf], writes=[Pp_buf])
                if pp >= 1:
                    for h2 in (0, 1):
                        emit_pv_pair(pp - 1, h2)
            for h2 in (0, 1):
                emit_pv_pair(npair - 1, h2)
            for h2 in (0, 1):
                pv_t, pv_buf = pvs[h2]
                rc_t, rc_buf = rcp[2 * (sbk % 2) + h2]
                sch.op("v", lambda eng, o=rc_t, i=pv_t:
                       eng.reciprocal(o[:], i[:].rearrange("p (j e) -> p j e", e=65)[:, :, 64]),
                       reads=[pv_buf], writes=[rc_buf])
                h = 2 * u + h2
                for jj in range(4):
                    tau = 4 * sbk + jj
                    sch.op("v", lambda eng, i=pv_t, jj=jj, tau=tau, h=h, rc=rc_t:
                           eng.tensor_scalar(aot[:, tau, 64 * h:64 * (h + 1)],
                                             i[:, 65 * jj:65 * jj + 64], rc[:, jj:jj + 1], None, ALU.mult),
                           reads=[pv_buf, rc_buf], writes=[aot_b])

    sch.drain_dmas()
    sch.emit_block()
    p2.close()

    p3 = ExitStack()
    af = [sb("af%d" % i, [128, 1024], BF16, p3) for i in range(2)]
    xin = [sb("xin%d" % i, [128, D], F32, p3) for i in range(2)]
    yv = [sb("y%d" % i, [128, D], F32, p3) for i in range(2)]
    x1t_ = [sb("x1t%d" % i, [128, D], F32, p3) for i in range(2)]
    x1bf = [sb("x1bf%d" % i, [128, D], BF16, p3) for i in range(2)]
    stats = [sb("stats%d" % i, [128, 18], F32, p3) for i in range(2)]
    atp = [ps("atp%d" % i, [128, 1024], BF16, p3) for i in range(2)]
    wop = [ps("wop%d" % i, [128, 1024], F32, p3) for i in range(2)]
    gcp, gcp_b = ps("gcp", [128, 8], F32, p3)

    def emit_gcsum(g, x1b_t, x1b_buf):
        for c in range(8):
            sch.op("t", lambda eng, g=g, c=c, xb=x1b_t:
                   eng.matmul(gcp[:, c:c + 1], xb[:, 128 * c:128 * (c + 1)], onescol[:],
                              start=(g == 0 and c == 0), stop=(g == NT - 1 and c == 7), skip_group_check=True),
                   reads=[x1b_buf, onescol_b], writes=[gcp_b])

    for g in range(NT):
        af_t, af_buf = af[g % 2]
        xin_t, xin_buf = xin[g % 2]
        y_t, y_buf = yv[g % 2]
        x1_t, x1_buf = x1t_[g % 2]
        x1b_t, x1b_buf = x1bf[g % 2]
        st_t, st_buf = stats[g % 2]
        atp_t, atp_buf = atp[g % 2]
        wop_t, wop_buf = wop[g % 2]
        load("q", xin_t[:], x_d[128 * g:128 * (g + 1), :], [xd_b], [xin_buf])
        for c in range(8):
            sch.op("t", lambda eng, o=atp_t, g=g, c=c:
                   eng.transpose(o[:, 128 * c:128 * (c + 1)], aot[:, g, 128 * c:128 * (c + 1)], identb[:]),
                   reads=[aot_b, identb_b], writes=[atp_buf])
        sch.op("v", lambda eng, o=af_t, i=atp_t: eng.tensor_copy(o[:], i[:]),
               reads=[atp_buf], writes=[af_buf])
        for n2 in range(2):
            for p in range(8):
                sch.op("t", lambda eng, o=wop_t, n2=n2, p=p, a=af_t:
                       eng.matmul(o[:, 512 * n2:512 * (n2 + 1)], a[:, 128 * p:128 * (p + 1)],
                                  wo[:, p, 512 * n2:512 * (n2 + 1)],
                                  start=(p == 0), stop=(p == 7), skip_group_check=True),
                       reads=[af_buf, wo_b], writes=[wop_buf])
        if g >= 1:
            emit_gcsum(g - 1, x1bf[(g - 1) % 2][0], x1bf[(g - 1) % 2][1])
        sch.op("v", lambda eng, o=y_t, i0=xin_t, i1=wop_t:
               eng.tensor_tensor(o[:], i0[:], i1[:], ALU.add),
               reads=[xin_buf, wop_buf], writes=[y_buf])
        sch.op("v", lambda eng, i=y_t, st=st_t: eng.bn_stats(st[:, 0:6], i[:, 0:512]),
               reads=[y_buf], writes=[st_buf])
        sch.op("v", lambda eng, i=y_t, st=st_t: eng.bn_stats(st[:, 6:12], i[:, 512:1024]),
               reads=[y_buf], writes=[st_buf])
        sch.op("v", lambda eng, st=st_t: eng.bn_aggr(st[:, 12:14], st[:, 0:12]),
               reads=[st_buf], writes=[st_buf])
        sch.op("v", lambda eng, st=st_t: eng.tensor_scalar(st[:, 14:15], st[:, 12:13], -1.0, None, ALU.mult),
               reads=[st_buf], writes=[st_buf])
        sch.op("s", lambda eng, st=st_t: eng.activation(st[:, 15:16], st[:, 13:14], AF.Sqrt, bias=epsc[:]),
               reads=[st_buf, epsc_b], writes=[st_buf])
        sch.op("v", lambda eng, st=st_t: eng.reciprocal(st[:, 16:17], st[:, 15:16]),
               reads=[st_buf], writes=[st_buf])
        sch.op("v", lambda eng, o=x1_t, i=y_t, st=st_t:
               eng.tensor_scalar(o[:], i[:], st[:, 14:15], st[:, 16:17], ALU.add, ALU.mult),
               reads=[y_buf, st_buf], writes=[x1_buf])
        sch.op("s", lambda eng, o=x1b_t, i=x1_t: eng.copy(o[:], i[:]),
               reads=[x1_buf], writes=[x1b_buf])
        load("q", x1_d[128 * g:128 * (g + 1), :], x1_t[:], [x1_buf], [x1_b])


    emit_gcsum(NT - 1, x1bf[(NT - 1) % 2][0], x1bf[(NT - 1) % 2][1])
    sch.op("v", lambda eng: eng.tensor_scalar(gcf[:], gcp[:], 1.0 / S, None, ALU.mult),
           reads=[gcp_b], writes=[gcf_b])
    sch.drain_dmas()
    sch.emit_block()
    p3.close()
    p23.close()

    p4w = ExitStack()
    cw1, cw1_b = sb("cw1", [128, 8, D], BF16, p4w)
    cw2, cw2_b = sb("cw2", [128, 8, 512], BF16, p4w)
    cw3, cw3_b = sb("cw3", [128, 4, 256], BF16, p4w)
    cw4, cw4_b = sb("cw4", [128, 2], BF16, p4w)
    gw1, gw1_b = sb("gw1", [128, 8, 512], BF16, p4w)
    gw2, gw2_b = sb("gw2", [128, 4, 256], BF16, p4w)
    gw3, gw3_b = sb("gw3", [128, 2, 128], BF16, p4w)
    load("p", gw1[:], gw1_d.rearrange("(c p) f -> p c f", p=128), [], [gw1_b])
    load("p", gw2[:], gw2_d.rearrange("(c p) f -> p c f", p=128), [], [gw2_b])
    load("p", gw3[:], gw3_d.rearrange("(c p) f -> p c f", p=128), [], [gw3_b])
    load("p", cw1[:], cw1_d.rearrange("(c p) f -> p c f", p=128), [], [cw1_b])
    load("p", cw2[:], cw2_d.rearrange("(c p) f -> p c f", p=128), [], [cw2_b])
    load("p", cw3[:], cw3_d.rearrange("(c p) f -> p c f", p=128), [], [cw3_b])
    load("p", cw4[:], cw4_d[:], [], [cw4_b])

    p4 = ExitStack()
    gcfb, gcfb_b = sb("gcfb", [128, 8], BF16, p4)
    h1g, h1g_b = sb("h1g", [128, 4], BF16, p4)
    h2g, h2g_b = sb("h2g", [128, 2], BF16, p4)
    gfv, gfv_b = sb("gfv", [128, 4], F32, p4)
    x1l = [sb("x1l%d" % i, [128, 4, D], F32, p4) for i in range(2)]
    quarters = [sb("qq%d" % i, [128, 256], F32, p4) for i in range(36)]
    ofs = [sb("of%d" % i, [128, 8, 512], BF16, p4) for i in range(2)]
    h1s = [sb("h1_%d" % i, [128, 8, 512], BF16, p4) for i in range(2)]
    h2s = [sb("h2_%d" % i, [128, 4, 512], BF16, p4) for i in range(2)]
    h3s = [sb("h3_%d" % i, [128, 2, 512], BF16, p4) for i in range(2)]
    tfcs = [sb("tfc%d" % i, [128, 8], F32, p4) for i in range(2)]
    yf = [sb("yf%d" % i, [128, D], F32, p4) for i in range(4)]
    ot = [sb("ot%d" % i, [128, D], F32, p4) for i in range(2)]
    st4 = [sb("st4_%d" % i, [128, 12], F32, p4) for i in range(2)]
    fin, fin_b = sb("fin", [128, 20], F32, p4)
    ofp = [ps("ofp%d" % i, [128, 512], F32, p4) for i in range(4)]
    cnp = [ps("cnp%d" % i, [128, 512], F32, p4) for i in range(2)]
    z4ps = [ps("z4p%d" % i, [128, 8], F32, p4) for i in range(2)]

    z4p, z4p_b = z4ps[0]
    sch.op("s", lambda eng: eng.copy(gcfb[:], gcf[:]), reads=[gcf_b], writes=[gcfb_b])
    for fo in range(4):
        for p in range(8):
            sch.op("t", lambda eng, fo=fo, p=p:
                   eng.matmul(z4p[:, fo:fo + 1], gw1[:, p, 128 * fo:128 * (fo + 1)], gcfb[:, p:p + 1],
                              start=(fo == 0 and p == 0), stop=(fo == 3 and p == 7), skip_group_check=True),
                   reads=[gw1_b, gcfb_b], writes=[z4p_b])
    for fo in range(4):
        sch.op("s", lambda eng, fo=fo:
               eng.activation(h1g[:, fo:fo + 1], z4p[:, fo:fo + 1], AF.Gelu, bias=g1s[:, fo:fo + 1]),
               reads=[z4p_b, g1s_b], writes=[h1g_b])
    for fo in range(2):
        for p in range(4):
            sch.op("t", lambda eng, fo=fo, p=p:
                   eng.matmul(z4p[:, 4 + fo:5 + fo], gw2[:, p, 128 * fo:128 * (fo + 1)], h1g[:, p:p + 1],
                              start=(fo == 0 and p == 0), stop=(fo == 1 and p == 3), skip_group_check=True),
                   reads=[gw2_b, h1g_b], writes=[z4p_b])
    for fo in range(2):
        sch.op("s", lambda eng, fo=fo:
               eng.activation(h2g[:, fo:fo + 1], z4p[:, 4 + fo:5 + fo], AF.Gelu, bias=g2s[:, fo:fo + 1]),
               reads=[z4p_b, g2s_b], writes=[h2g_b])
    for p in range(2):
        sch.op("t", lambda eng, p=p:
               eng.matmul(z4p[:, 6:7], gw3[:, p, :], h2g[:, p:p + 1],
                          start=(p == 0), stop=(p == 1), skip_group_check=True),
               reads=[gw3_b, h2g_b], writes=[z4p_b])
    sch.op("v", lambda eng: eng.tensor_scalar(gfv[:, 1:2], g3s[:], 0.5, None, ALU.mult),
           reads=[g3s_b], writes=[gfv_b])
    sch.op("s", lambda eng: eng.activation(gfv[:, 0:1], z4p[:, 6:7], AF.Tanh, bias=gfv[:, 1:2], scale=0.5),
           reads=[z4p_b, gfv_b], writes=[gfv_b])
    sch.op("v", lambda eng: eng.tensor_scalar(gfv[:, 1:2], gfv[:, 0:1], 0.7, -0.3, ALU.mult, ALU.add),
           reads=[gfv_b], writes=[gfv_b])
    sch.op("v", lambda eng: eng.tensor_scalar(gfv[:, 2:3], gfv[:, 1:2], ars[:], 0.1, ALU.mult, ALU.add),
           reads=[gfv_b, ars_b], writes=[gfv_b])
    sch.op("v", lambda eng: eng.tensor_scalar(gfv[:, 3:4], ars[:], 0.3, None, ALU.mult),
           reads=[ars_b], writes=[gfv_b])
    sch.op("v", lambda eng: eng.tensor_tensor(gfv[:, 2:3], gfv[:, 2:3], gfv[:, 3:4], ALU.add),
           reads=[gfv_b], writes=[gfv_b])
    b4h, b4h_b = sb("b4h", [128, 1], F32, p4)
    sch.op("v", lambda eng: eng.tensor_scalar(b4h[:], b4s[:], 0.5, None, ALU.mult),
           reads=[b4s_b], writes=[b4h_b])

    x1_src = x1_d.rearrange("(b t4 p) e -> b p t4 e", p=128, t4=4)
    qpool = [quarters[18 * par:18 * par + 18] for par in range(2)]
    qbs = {}

    def stage_block(b):
        par = b % 2
        x1l_t, x1l_buf = x1l[par]
        load("q", x1l_t[:], x1_src[b], [x1_b], [x1l_buf])
        qb = [[None] * 4 for _ in range(4)]
        pool = qpool[par]
        idx = 0
        for tau in range(4):
            for qt in range(4):
                qb[tau][qt] = pool[idx]
                idx += 1
        for tau in range(4):
            for qt in range(4):
                t, bfq = qb[tau][qt]
                sch.op("s", lambda eng, t=t, tau=tau, qt=qt, x=x1l_t:
                       eng.copy(t[:], x[:, tau, 256 * qt:256 * (qt + 1)]),
                       reads=[x1l_buf], writes=[bfq])
        qbs[b] = (qb, [pool[16], pool[17]])

    def emit_finale(b, qb, x1l_t, x1l_buf):
        for tau in range(4):
            y_t, y_buf = yf[tau]
            st_t, st_buf = st4[tau % 2]
            for qt in range(4):
                t, bfq = qb[tau][qt]
                sch.op("v", lambda eng, o=y_t, qt=qt, t=t, x=x1l_t, tau=tau:
                       eng.tensor_tensor(o[:, 256 * qt:256 * (qt + 1)], t[:], x[:, tau, 256 * qt:256 * (qt + 1)],
                                         ALU.add),
                       reads=[bfq, x1l_buf], writes=[y_buf])
            sch.op("v", lambda eng, i=y_t, st=st_t: eng.bn_stats(st[:, 0:6], i[:, 0:512]),
                   reads=[y_buf], writes=[st_buf])
            sch.op("v", lambda eng, i=y_t, st=st_t: eng.bn_stats(st[:, 6:12], i[:, 512:1024]),
                   reads=[y_buf], writes=[st_buf])
            sch.op("v", lambda eng, st=st_t, tau=tau: eng.bn_aggr(fin[:, 2 * tau:2 * tau + 2], st[:, 0:12]),
                   reads=[st_buf], writes=[fin_b])
        sch.op("v", lambda eng: eng.tensor_scalar(
            fin[:, 8:12], fin[:].rearrange("p (t k) -> p k t", k=2)[:, 0, 0:4], -1.0, None, ALU.mult),
            reads=[fin_b], writes=[fin_b])
        sch.op("s", lambda eng: eng.activation(
            fin[:, 12:16], fin[:].rearrange("p (t k) -> p k t", k=2)[:, 1, 0:4], AF.Sqrt, bias=epsc[:]),
            reads=[fin_b, epsc_b], writes=[fin_b])
        sch.op("v", lambda eng: eng.reciprocal(fin[:, 16:20], fin[:, 12:16]),
               reads=[fin_b], writes=[fin_b])
        for tau in range(4):
            g = 4 * b + tau
            y_t, y_buf = yf[tau]
            o_t, o_buf = ot[tau % 2]
            sch.op("v", lambda eng, o=o_t, i=y_t, tau=tau:
                   eng.tensor_scalar(o[:], i[:], fin[:, 8 + tau:9 + tau], fin[:, 16 + tau:17 + tau],
                                     ALU.add, ALU.mult),
                   reads=[y_buf, fin_b], writes=[o_buf])
            load("q", out_d[128 * g:128 * (g + 1), :], o_t[:], [o_buf], [out_b])

    pending = [None]
    stage_block(0)
    for b in range(NB):
        par = b % 2
        x1l_t, x1l_buf = x1l[par]
        qb, spares = qbs.pop(b)
        of, of_b = ofs[par]
        h1, h1_b = h1s[par]
        h2, h2_b = h2s[par]
        h3, h3_b = h3s[par]
        tfc, tfc_b = tfcs[par]
        z4p, z4p_b = z4ps[par]
        for cyc in range(3):
            for c in range(8):
                ofp_t, ofp_buf = ofp[c % 4]
                qt, off = c // 2, 128 * (c % 2)
                for tau in range(4):
                    t, bfq = qb[tau][qt]
                    sch.op("t", lambda eng, o=ofp_t, t=t, tau=tau, off=off:
                           eng.transpose(o[:, 128 * tau:128 * (tau + 1)], t[:, off:off + 128], identf[:]),
                           reads=[bfq, identf_b], writes=[ofp_buf])
                sch.op("v", lambda eng, c=c, s=ofp_t, of=of: eng.tensor_copy(of[:, c, :], s[:]),
                       reads=[ofp_buf], writes=[of_b])
            for fo in range(8):
                cn_t, cn_buf = cnp[fo % 2]
                for p in range(8):
                    sch.op("t", lambda eng, o=cn_t, fo=fo, p=p, of=of:
                           eng.matmul(o[:], cw1[:, p, 128 * fo:128 * (fo + 1)], of[:, p, :],
                                      start=(p == 0), stop=(p == 7), skip_group_check=True),
                           reads=[cw1_b, of_b], writes=[cn_buf])
                sch.op("s", lambda eng, o=h1, fo=fo, i=cn_t:
                       eng.activation(o[:, fo, :], i[:], AF.Gelu, bias=b1s[:, fo:fo + 1]),
                       reads=[cn_buf, b1s_b], writes=[h1_b])
            if cyc == 0 and pending[0] is not None:
                emit_finale(*pending[0])
                pending[0] = None
            for fo in range(4):
                cn_t, cn_buf = cnp[fo % 2]
                for p in range(8):
                    sch.op("t", lambda eng, o=cn_t, fo=fo, p=p, h1=h1:
                           eng.matmul(o[:], cw2[:, p, 128 * fo:128 * (fo + 1)], h1[:, p, :],
                                      start=(p == 0), stop=(p == 7), skip_group_check=True),
                           reads=[cw2_b, h1_b], writes=[cn_buf])
                sch.op("s", lambda eng, o=h2, fo=fo, i=cn_t:
                       eng.activation(o[:, fo, :], i[:], AF.Gelu, bias=b2s[:, fo:fo + 1]),
                       reads=[cn_buf, b2s_b], writes=[h2_b])
            for fo in range(2):
                cn_t, cn_buf = cnp[fo % 2]
                for p in range(4):
                    sch.op("t", lambda eng, o=cn_t, fo=fo, p=p, h2=h2:
                           eng.matmul(o[:], cw3[:, p, 128 * fo:128 * (fo + 1)], h2[:, p, :],
                                      start=(p == 0), stop=(p == 3), skip_group_check=True),
                           reads=[cw3_b, h2_b], writes=[cn_buf])
                sch.op("s", lambda eng, o=h3, fo=fo, i=cn_t:
                       eng.activation(o[:, fo, :], i[:], AF.Gelu, bias=b3s[:, fo:fo + 1]),
                       reads=[cn_buf, b3s_b], writes=[h3_b])
            for tau in range(4):
                for p in range(2):
                    sch.op("t", lambda eng, tau=tau, p=p, h3=h3, z=z4p:
                           eng.matmul(z[:, tau:tau + 1], h3[:, p, 128 * tau:128 * (tau + 1)], cw4[:, p:p + 1],
                                      start=(tau == 0 and p == 0), stop=(tau == 3 and p == 1), skip_group_check=True),
                           reads=[h3_b, cw4_b], writes=[z4p_b])
            sch.op("s", lambda eng, tfc=tfc, z=z4p: eng.activation(tfc[:, 0:4], z[:, 0:4], AF.Tanh,
                                                                   bias=b4h[:], scale=0.5),
                   reads=[z4p_b, b4h_b], writes=[tfc_b])
            sch.op("v", lambda eng, tfc=tfc: eng.tensor_scalar(tfc[:, 0:4], tfc[:, 0:4], gfv[:, 3:4], gfv[:, 2:3],
                                                               ALU.mult, ALU.add),
                   reads=[tfc_b, gfv_b], writes=[tfc_b])
            sch.op("v", lambda eng, tfc=tfc: eng.tensor_scalar(tfc[:, 4:8], tfc[:, 0:4], -1.0, None, ALU.mult),
                   reads=[tfc_b], writes=[tfc_b])
            for tau in range(4):
                for (qa, qc, sgn) in ((0, 2, +1), (1, 3, -1)):
                    ta, ba = qb[tau][qa]
                    tc_, bc = qb[tau][qc]
                    s1t, s1b = spares[0]
                    s2t, s2b = spares[1]
                    ca = tfc[:, (tau if sgn > 0 else 4 + tau):(tau if sgn > 0 else 4 + tau) + 1]
                    cb = tfc[:, (4 + tau if sgn > 0 else tau):(4 + tau if sgn > 0 else tau) + 1]
                    sch.op("v", lambda eng, o=s1t, i0=tc_, sc=ca, i1=ta:
                           eng.scalar_tensor_tensor(o[:], i0[:], sc, i1[:], ALU.mult, ALU.add),
                           reads=[bc, ba, tfc_b], writes=[s1b])
                    sch.op("v", lambda eng, o=s2t, i0=ta, sc=cb, i1=tc_:
                           eng.scalar_tensor_tensor(o[:], i0[:], sc, i1[:], ALU.mult, ALU.add),
                           reads=[ba, bc, tfc_b], writes=[s2b])
                    qb[tau][qa] = spares[0]
                    qb[tau][qc] = spares[1]
                    spares = [(ta, ba), (tc_, bc)]
        pending[0] = (b, qb, x1l_t, x1l_buf)
        if b + 1 < NB:
            stage_block(b + 1)
    emit_finale(*pending[0])

    sch.drain_dmas()
    sch.emit_block()
    p4.close()
    p4w.close()
    root.close()
    return nc


def _host_constants():
    pos = np.arange(S, dtype=np.float64)
    cos = np.ones((128, S), dtype=np.float64)
    sin = np.zeros((128, S), dtype=np.float64)
    for p in range(128):
        dd = p % 64
        if dd < ROPE:
            j = dd // 2
            invf = 10000.0 ** (-(2.0 * j) / ROPE)
            ang = pos * invf
            cos[p] = np.cos(ang)
            sin[p] = np.sin(ang)
    r128 = np.zeros((128, 128), dtype=np.float32)
    for blk in range(2):
        o = 64 * blk
        for j in range(ROPE // 2):
            r128[o + 2 * j + 1, o + 2 * j] = -1.0
            r128[o + 2 * j, o + 2 * j + 1] = 1.0
    tri = (np.arange(128)[:, None] <= np.arange(128)[None, :]).astype(np.float32)
    return (cos.astype(np.float32), sin.astype(np.float32),
            r128.astype(BF), tri.astype(BF))


_NC_CACHE = [None]


def kernel(L0_attn_hidden_states, L0_attn_w_qkv, L0_attn_w_out,
           L0_ffn_cn_w1, L0_ffn_cn_b1, L0_ffn_cn_w2, L0_ffn_cn_b2,
           L0_ffn_cn_w3, L0_ffn_cn_b3, L0_ffn_cn_w4, L0_ffn_cn_b4,
           L0_ffn_gc_w1, L0_ffn_gc_b1, L0_ffn_gc_w2, L0_ffn_gc_b2,
           L0_ffn_gc_w3, L0_ffn_gc_b3, L0_ffn_adaptive_range):
    x = np.ascontiguousarray(np.asarray(L0_attn_hidden_states, dtype=np.float32))
    w_qkv = np.asarray(L0_attn_w_qkv, dtype=np.float32)
    w_out = np.asarray(L0_attn_w_out, dtype=np.float32)

    perm = np.empty(3 * D, dtype=np.int64)
    for c in range(3 * D):
        kind, r = divmod(c, D)
        h, dd = divmod(r, HD)
        base = (h // 4) * 768 + (h % 4) * 64 + dd
        perm[c] = base + (0 if kind == 0 else (512 if kind == 1 else 256))
    wqkvT = np.ascontiguousarray(w_qkv[perm, :].T)
    woutT = np.ascontiguousarray(w_out.T)

    rep = lambda a: np.ascontiguousarray(np.repeat(np.asarray(a, np.float32).reshape(1, -1), 128, axis=0))
    colmaj = lambda a, k: np.ascontiguousarray(np.asarray(a, np.float32).reshape(k, 128).T)

    cos_f, sin_f, r128, tri = _host_constants()
    in_common = dict(
        wqkvT=wqkvT, woutT=woutT,
        cnw1=np.ascontiguousarray(np.asarray(L0_ffn_cn_w1, np.float32)),
        cnw2=np.ascontiguousarray(np.asarray(L0_ffn_cn_w2, np.float32)),
        cnw3=np.ascontiguousarray(np.asarray(L0_ffn_cn_w3, np.float32)),
        cnw4=colmaj(L0_ffn_cn_w4, 2),
        gcw1=np.ascontiguousarray(np.asarray(L0_ffn_gc_w1, np.float32)),
        gcw2=np.ascontiguousarray(np.asarray(L0_ffn_gc_w2, np.float32)),
        gcw3r=np.ascontiguousarray(np.repeat(np.asarray(L0_ffn_gc_w3, np.float32).reshape(256, 1), 128, axis=1)),
        b1=colmaj(L0_ffn_cn_b1, 8), b2=colmaj(L0_ffn_cn_b2, 4), b3=colmaj(L0_ffn_cn_b3, 2),
        b4=rep(L0_ffn_cn_b4).reshape(128, 1),
        g1=colmaj(L0_ffn_gc_b1, 4), g2=colmaj(L0_ffn_gc_b2, 2),
        g3=rep(L0_ffn_gc_b3).reshape(128, 1),
        ar=rep(L0_ffn_adaptive_range).reshape(128, 1),
        identb=np.eye(128, dtype=BF), identf=np.eye(128, dtype=np.float32),
        tri=tri, onescol=np.ones((128, 1), dtype=BF),
        cosf=cos_f, sinf=sin_f, r128t=r128,
    )
    if _NC_CACHE[0] is None:
        _NC_CACHE[0] = build_nc()
    nc = _NC_CACHE[0]
    in_maps = [dict(in_common, x=np.ascontiguousarray(x[i])) for i in range(8)]
    res = run_bass_kernel_spmd(nc, in_maps, core_ids=list(range(8)))
    return np.stack([np.asarray(r["out"], dtype=np.float32) for r in res.results], axis=0)


if __name__ == "__main__":
    np.random.seed(0)
    import reference as _ref
    inputs = _ref.setup_inputs(0)
    got = kernel(**{k: np.asarray(v) for k, v in inputs.items()})
    want = np.asarray(_ref.reference(**inputs))
    num = float(np.mean((got - want) ** 2))
    den = float(np.mean(want ** 2))
    print("resid-var ratio:", num / den)


# version 16
# speedup vs baseline: 1.1372x; 1.1372x over previous
import numpy as np
import ml_dtypes
from contextlib import ExitStack

import concourse.bass as bass
import concourse.mybir as mybir
from concourse.bass_utils import run_bass_kernel_spmd

F32 = mybir.dt.float32
BF16 = mybir.dt.bfloat16
AF = mybir.ActivationFunctionType
ALU = mybir.AluOpType
AX = mybir.AxisListType

S = 4096
D = 1024
H = 16
HD = 64
ROPE = 32
NB = 8
NT = 32
EPS = 1e-5
BF = ml_dtypes.bfloat16


class Buf:
    __slots__ = ("w", "r")

    def __init__(self):
        self.w = None
        self.r = {}


class Sched:
    ENGNAME = {"t": "tensor", "v": "vector", "s": "scalar", "p": "gpsimd", "q": "sync"}

    NL = 8

    def __init__(self, nc, stack):
        self.nc = nc
        self.semkeys = (["t", "v", "s", "p"]
                        + ["q%d" % i for i in range(self.NL)]
                        + ["p%d" % i for i in range(self.NL)]
                        + ["s%d" % i for i in range(4)])
        self.sems = {k: stack.enter_context(nc.semaphore("sem_" + k)) for k in self.semkeys}
        self.cnt = {k: 0 for k in self.semkeys}
        self.waited = {e: {k: 0 for k in self.semkeys} for e in "tvspq"}
        self.thunks = {e: [] for e in "tvspq"}
        self.dmaidx = {"q": 0, "p": 0, "s": 0}

    def _wait(self, e, semkey, val):
        if val <= 0:
            return
        if e == "t" and semkey == "t":
            return
        if self.waited[e][semkey] >= val:
            return
        self.waited[e][semkey] = val
        sem = self.sems[semkey]
        self.thunks[e].append(lambda eng, s=sem, v=val: eng.wait_ge(s, v))

    def op(self, e, emit, reads=(), writes=(), dma=False):
        for b in reads:
            if b.w is not None:
                self._wait(e, b.w[0], b.w[1])
        for b in writes:
            if b.w is not None:
                self._wait(e, b.w[0], b.w[1])
            for sk, v in b.r.items():
                self._wait(e, sk, v)
        if dma:
            nl = 4 if e == "s" else self.NL
            lane = self.dmaidx[e] % nl
            self.dmaidx[e] += 1
            semkey = "%s%d" % (e, lane)
            inc = 16
            self._wait(e, semkey, self.cnt[semkey])
        else:
            semkey = e
            inc = 1
        self.cnt[semkey] += inc
        tag = (semkey, self.cnt[semkey])
        sem = self.sems[semkey]
        self.thunks[e].append(lambda eng, f=emit, s=sem, i=inc: f(eng).then_inc(s, i))
        for b in writes:
            b.w = tag
            b.r = {}
        for b in reads:
            if b.r.get(tag[0], 0) < tag[1]:
                b.r[tag[0]] = tag[1]
        return tag

    def drain_dmas(self):
        for i in range(self.NL):
            self._wait("q", "q%d" % i, self.cnt["q%d" % i])
            self._wait("q", "p%d" % i, self.cnt["p%d" % i])
        for i in range(4):
            self._wait("q", "s%d" % i, self.cnt["s%d" % i])

    def emit_block(self):
        with self.nc.Block() as blk:
            for e in "tvspq":
                if not self.thunks[e]:
                    continue

                def runner(eng, ths=self.thunks[e]):
                    for th in ths:
                        th(eng)

                getattr(blk, self.ENGNAME[e])(runner)
        self.thunks = {e: [] for e in "tvspq"}


def build_nc():
    nc = bass.Bass("TRN2", target_bir_lowering=False)

    x_d = nc.dram_tensor("x", [S, D], F32, kind="ExternalInput").ap()
    wq_d = nc.dram_tensor("wqkvT", [D, 3 * D], F32, kind="ExternalInput").ap()
    wo_d = nc.dram_tensor("woutT", [D, D], F32, kind="ExternalInput").ap()
    cw1_d = nc.dram_tensor("cnw1", [D, D], F32, kind="ExternalInput").ap()
    cw2_d = nc.dram_tensor("cnw2", [D, 512], F32, kind="ExternalInput").ap()
    cw3_d = nc.dram_tensor("cnw3", [512, 256], F32, kind="ExternalInput").ap()
    cw4_d = nc.dram_tensor("cnw4", [128, 2], F32, kind="ExternalInput").ap()
    gw1_d = nc.dram_tensor("gcw1", [D, 512], F32, kind="ExternalInput").ap()
    gw2_d = nc.dram_tensor("gcw2", [512, 256], F32, kind="ExternalInput").ap()
    gw3_d = nc.dram_tensor("gcw3r", [256, 128], F32, kind="ExternalInput").ap()
    b1_d = nc.dram_tensor("b1", [128, 8], F32, kind="ExternalInput").ap()
    b2_d = nc.dram_tensor("b2", [128, 4], F32, kind="ExternalInput").ap()
    b3_d = nc.dram_tensor("b3", [128, 2], F32, kind="ExternalInput").ap()
    b4_d = nc.dram_tensor("b4", [128, 1], F32, kind="ExternalInput").ap()
    g1_d = nc.dram_tensor("g1", [128, 4], F32, kind="ExternalInput").ap()
    g2_d = nc.dram_tensor("g2", [128, 2], F32, kind="ExternalInput").ap()
    g3_d = nc.dram_tensor("g3", [128, 1], F32, kind="ExternalInput").ap()
    ar_d = nc.dram_tensor("ar", [128, 1], F32, kind="ExternalInput").ap()
    idb_d = nc.dram_tensor("identb", [128, 128], BF16, kind="ExternalInput").ap()
    idf_d = nc.dram_tensor("identf", [128, 128], F32, kind="ExternalInput").ap()
    tri_d = nc.dram_tensor("tri", [128, 128], BF16, kind="ExternalInput").ap()
    ones_d = nc.dram_tensor("onescol", [128, 1], BF16, kind="ExternalInput").ap()
    cos_d = nc.dram_tensor("cosf", [128, S], F32, kind="ExternalInput").ap()
    sin_d = nc.dram_tensor("sinf", [128, S], F32, kind="ExternalInput").ap()
    r128_d = nc.dram_tensor("r128t", [128, 128], BF16, kind="ExternalInput").ap()

    qf_d = nc.dram_tensor("qf", [D, S], BF16, kind="Internal").ap()
    kf_d = nc.dram_tensor("kf", [D, S], BF16, kind="Internal").ap()
    vt_d = nc.dram_tensor("vt", [S, D], BF16, kind="Internal").ap()
    x1_d = nc.dram_tensor("x1", [S, D], F32, kind="Internal").ap()
    out_d = nc.dram_tensor("out", [S, D], F32, kind="ExternalOutput").ap()

    root = ExitStack()
    sch = Sched(nc, root)

    def sb(name, shape, dtype, stk):
        t = stk.enter_context(nc.sbuf_tensor("sb_" + name, shape, dtype))
        return t, Buf()

    def ps(name, shape, dtype, stk):
        t = stk.enter_context(nc.psum_tensor("ps_" + name, shape, dtype))
        return t, Buf()

    qf_b, kf_b, vt_b, x1_b, out_b = Buf(), Buf(), Buf(), Buf(), Buf()
    xd_b = Buf()

    identb, identb_b = sb("identb", [128, 128], BF16, root)
    identf, identf_b = sb("identf", [128, 128], F32, root)
    tri, tri_b = sb("tri", [128, 128], BF16, root)
    onescol, onescol_b = sb("onescol", [128, 1], BF16, root)
    b1s, b1s_b = sb("b1s", [128, 8], F32, root)
    b2s, b2s_b = sb("b2s", [128, 4], F32, root)
    b3s, b3s_b = sb("b3s", [128, 2], F32, root)
    b4s, b4s_b = sb("b4s", [128, 1], F32, root)
    g1s, g1s_b = sb("g1s", [128, 4], F32, root)
    g2s, g2s_b = sb("g2s", [128, 2], F32, root)
    g3s, g3s_b = sb("g3s", [128, 1], F32, root)
    ars, ars_b = sb("ars", [128, 1], F32, root)
    epsc, epsc_b = sb("epsc", [128, 1], F32, root)
    gcf, gcf_b = sb("gcf", [128, 8], F32, root)

    def load(e, dst_ap, src_ap, rbufs, wbufs):
        sch.op(e, lambda eng, d=dst_ap, s=src_ap: eng.dma_start(d, s),
               reads=rbufs, writes=wbufs, dma=True)

    for dst, dstb, src in [
        (identb, identb_b, idb_d), (identf, identf_b, idf_d), (tri, tri_b, tri_d),
        (onescol, onescol_b, ones_d), (b1s, b1s_b, b1_d), (b2s, b2s_b, b2_d),
        (b3s, b3s_b, b3_d), (b4s, b4s_b, b4_d), (g1s, g1s_b, g1_d),
        (g2s, g2s_b, g2_d), (g3s, g3s_b, g3_d), (ars, ars_b, ar_d),
    ]:
        load("q", dst[:], src[:], [], [dstb])
    sch.op("v", lambda eng: eng.memset(epsc[:], EPS), reads=[], writes=[epsc_b])

    p1 = ExitStack()
    wq, wq_b = sb("wq", [128, 8, 3 * D], BF16, p1)
    cosf, cosf_b = sb("cosf", [128, S], F32, p1)
    sinf, sinf_b = sb("sinf", [128, S], F32, p1)
    r128, r128_b = sb("r128", [128, 128], BF16, p1)
    load("q", r128[:], r128_d[:], [], [r128_b])
    load("q", cosf[:], cos_d[:], [], [cosf_b])
    load("q", sinf[:], sin_d[:], [], [sinf_b])
    wq_src = wq_d.rearrange("(c p) f -> p c f", p=128)
    load("p", wq[:, 0:4, :], wq_src[:, 0:4, :], [], [wq_b])
    load("p", wq[:, 4:8, :], wq_src[:, 4:8, :], [], [wq_b])

    xt = [sb("xt%d" % i, [128, D], F32, p1) for i in range(2)]
    xtb = [sb("xtb%d" % i, [128, D], BF16, p1) for i in range(2)]
    xf = [sb("xf%d" % i, [128, 8, 512], BF16, p1) for i in range(2)]
    qkraw, qkraw_b = sb("qkraw", [128, 16, 512], BF16, p1)
    qkraw_bufs = [Buf() for _ in range(16)]
    qkro = [sb("qkro%d" % i, [128, 16, 512], BF16, p1) for i in range(2)]
    vraw = [sb("vraw%d" % i, [128, 512], BF16, p1) for i in range(2)]
    vsb, vsb_b = sb("vsb", [128, 8, 4, 128], BF16, p1)
    tsin = [sb("tsin%d" % i, [128, 512], F32, p1) for i in range(2)]
    tcos = [sb("tcos%d" % i, [128, 512], F32, p1) for i in range(2)]

    trp = [ps("trp%d" % i, [128, 1024], BF16, p1) for i in range(2)]
    qps = [ps("qps%d" % i, [128, 512], F32, p1) for i in range(3)]
    rp = [ps("rp%d" % i, [128, 512], F32, p1) for i in range(2)]
    vtp, vtp_b = ps("vtp", [128, 512], BF16, p1)

    for b in range(NB):
        xfb, xfb_buf = xf[b % 2]
        qkro_t, qkro_buf = qkro[b % 2]
        for tau in range(4):
            g = 4 * b + tau
            xt_t, xt_buf = xt[g % 2]
            xtb_t, xtb_buf = xtb[g % 2]
            trp_t, trp_buf = trp[g % 2]
            load("q", xt_t[:], x_d[128 * g:128 * (g + 1), :], [xd_b], [xt_buf])
            sch.op("s", lambda eng, o=xtb_t, i=xt_t: eng.copy(o[:], i[:]),
                   reads=[xt_buf], writes=[xtb_buf])
            for c in range(8):
                sch.op("t", lambda eng, o=trp_t, i=xtb_t, c=c:
                       eng.transpose(o[:, 128 * c:128 * (c + 1)], i[:, 128 * c:128 * (c + 1)], identb[:]),
                       reads=[xtb_buf, identb_b], writes=[trp_buf])
            dst = xfb[:].rearrange("p c (t4 e) -> p c t4 e", t4=4)[:, :, tau, :]
            src = trp_t[:].rearrange("p (c e) -> p c e", c=8)
            sch.op("v", lambda eng, d=dst, s=src: eng.tensor_copy(d, s),
                   reads=[trp_buf], writes=[xfb_buf])
        for fo in range(24):
            qp_t, qp_buf = qps[fo % 3]
            for p in range(8):
                sch.op("t", lambda eng, o=qp_t, fo=fo, p=p, x=xfb:
                       eng.matmul(o[:], wq[:, p, 128 * fo:128 * (fo + 1)], x[:, p, :],
                                  start=(p == 0), stop=(p == 7), skip_group_check=True),
                       reads=[wq_b, xfb_buf], writes=[qp_buf])
            if fo < 16:
                c = fo
                sch.op("v", lambda eng, c=c, i=qp_t: eng.tensor_copy(qkraw[:, c, :], i[:]),
                       reads=[qp_buf], writes=[qkraw_bufs[c]])
            else:
                c = fo - 16
                vr_t, vr_buf = vraw[c % 2]
                sch.op("v", lambda eng, o=vr_t, i=qp_t: eng.tensor_copy(o[:], i[:]),
                       reads=[qp_buf], writes=[vr_buf])
                for tau in range(4):
                    sch.op("t", lambda eng, i=vr_t, tau=tau:
                           eng.transpose(vtp[:, 128 * tau:128 * (tau + 1)], i[:, 128 * tau:128 * (tau + 1)], identb[:]),
                           reads=[vr_buf, identb_b], writes=[vtp_b])
                sch.op("v", lambda eng, c=c: eng.tensor_copy(
                    vsb[:, c, :, :], vtp[:].rearrange("p (t4 e) -> p t4 e", t4=4)),
                    reads=[vtp_b], writes=[vsb_b])
        for c in range(16):
            rp_t, rp_buf = rp[c % 2]
            ts_t, ts_buf = tsin[c % 2]
            tc_t, tc_buf = tcos[c % 2]
            sch.op("t", lambda eng, o=rp_t, c=c:
                   eng.matmul(o[:], r128[:], qkraw[:, c, :], start=True, stop=True, skip_group_check=True),
                   reads=[r128_b, qkraw_bufs[c]], writes=[rp_buf])
            sch.op("v", lambda eng, o=ts_t, i=rp_t, b=b:
                   eng.tensor_tensor(o[:], i[:], sinf[:, 512 * b:512 * (b + 1)], ALU.mult),
                   reads=[rp_buf, sinf_b], writes=[ts_buf])
            sch.op("p", lambda eng, o=tc_t, c=c, b=b:
                   eng.tensor_tensor(o[:], qkraw[:, c, :], cosf[:, 512 * b:512 * (b + 1)], ALU.mult),
                   reads=[qkraw_bufs[c], cosf_b], writes=[tc_buf])
            sch.op("p", lambda eng, o=qkro_t, c=c, i0=tc_t, i1=ts_t:
                   eng.tensor_tensor(o[:, c, :], i0[:], i1[:], ALU.add),
                   reads=[tc_buf, ts_buf], writes=[qkro_buf])
        qdst = qf_d.rearrange("(c p) s -> p c s", p=128)[:, :, 512 * b:512 * (b + 1)]
        kdst = kf_d.rearrange("(c p) s -> p c s", p=128)[:, :, 512 * b:512 * (b + 1)]
        vdst = vt_d.rearrange("(B t4 p) (c e) -> B p c t4 e", t4=4, p=128, e=128)[b]
        sch.op("p", lambda eng, d=qdst, s=qkro_t: eng.dma_start(d, s[:, 0:8, :]),
               reads=[qkro_buf], writes=[qf_b], dma=True)
        sch.op("p", lambda eng, d=kdst, s=qkro_t: eng.dma_start(d, s[:, 8:16, :]),
               reads=[qkro_buf], writes=[kf_b], dma=True)
        sch.op("p", lambda eng, d=vdst: eng.dma_start(d, vsb[:]),
               reads=[vsb_b], writes=[vt_b], dma=True)

    sch.drain_dmas()
    sch.emit_block()
    p1.close()

    p23 = ExitStack()
    aot, aot_b = sb("aot", [128, NT, D], BF16, p23)
    wo, wo_b = sb("wo", [128, 8, D], BF16, p23)
    load("p", wo[:], wo_d.rearrange("(c p) f -> p c f", p=128), [], [wo_b])

    p2 = ExitStack()
    kst = [sb("kst%d" % i, [128, S], BF16, p2) for i in range(4)]
    qst = [sb("qst%d" % i, [128, S], BF16, p2) for i in range(4)]
    vst = [sb("vst%d" % i, [128, NT, 130], BF16, p2) for i in range(2)]
    Ppair = [[sb("Pp%d_%d" % (h2, i), [128, 1024], BF16, p2) for i in range(3)] for h2 in range(2)]
    rcp = [sb("rcp%d" % i, [128, 4], F32, p2) for i in range(4)]
    pairps = [ps("pair%d" % i, [128, 1024], F32, p2) for i in range(2)]
    pvps = [[ps("pv%d_%d" % (h2, i), [128, 260], F32, p2) for i in range(2)] for h2 in range(2)]

    for i in range(4):
        t, bfk = kst[i]
        sch.op("v", lambda eng, t=t: eng.memset(t[64:128, :], 0.0), reads=[], writes=[bfk])
        t, bfq2 = qst[i]
        sch.op("v", lambda eng, t=t: eng.memset(t[64:128, :], 0.0), reads=[], writes=[bfq2])
    for i in range(2):
        t, bfv = vst[i]
        sch.op("v", lambda eng, t=t: eng.memset(t[:].rearrange("p i (h2 e) -> p i h2 e", e=65)[:, :, :, 64], 1.0),
               reads=[], writes=[bfv])

    vt_src = vt_d.rearrange("(i p) e -> p i e", p=128)
    for u in range(H // 2):
        ub = u % 2
        ks = [kst[2 * ub], kst[2 * ub + 1]]
        qs = [qst[2 * ub], qst[2 * ub + 1]]
        vst_t, vst_buf = vst[ub]
        for h2 in (0, 1):
            h = 2 * u + h2
            load("q", ks[h2][0][0:64, :], kf_d[64 * h:64 * (h + 1), :], [kf_b], [ks[h2][1]])
            load("q", qs[h2][0][0:64, :], qf_d[64 * h:64 * (h + 1), :], [qf_b], [qs[h2][1]])
        vv = vst_t[:].rearrange("p i (h2 e) -> p i h2 e", e=65)
        load("q", vv[:, :, 0, 0:64], vt_src[:, :, 128 * u:128 * u + 64], [vt_b], [vst_buf])
        load("q", vv[:, :, 1, 0:64], vt_src[:, :, 128 * u + 64:128 * u + 128], [vt_b], [vst_buf])
        for sbk in range(NB):
            nI = 4 * sbk + 4
            npair = nI // 2
            pvs = [pvps[0][sbk % 2], pvps[1][sbk % 2]]
            first = [True, True]

            def emit_pv_pair(pp, h2, sbk=sbk, nI=nI, pvs=pvs, first=first,
                             vst_t=vst_t, vst_buf=vst_buf):
                Pp_t, Pp_buf = Ppair[h2][pp % 3]
                pv_t, pv_buf = pvs[h2]
                for half in (0, 1):
                    i = 2 * pp + half
                    for jj in range(4):
                        if i > 4 * sbk + jj:
                            continue
                        st0 = first[h2]
                        first[h2] = False
                        last = (i == nI - 1) and (jj == 3)
                        sch.op("t", lambda eng, o=pv_t, P=Pp_t, half=half, jj=jj, i=i, h2=h2,
                               st0=st0, last=last, v=vst_t:
                               eng.matmul(o[:, 65 * jj:65 * jj + 65],
                                          P[:, 512 * half + 128 * jj:512 * half + 128 * (jj + 1)],
                                          v[:, i, 65 * h2:65 * h2 + 65],
                                          start=st0, stop=last, skip_group_check=True),
                               reads=[Pp_buf, vst_buf], writes=[pv_buf])

            for pp in range(npair):
                for h2 in (0, 1):
                    pa_t, pa_buf = pairps[h2]
                    Pp_t, Pp_buf = Ppair[h2][pp % 3]
                    kst_t, kst_buf = ks[h2]
                    qst_t, qst_buf = qs[h2]
                    for half in (0, 1):
                        i = 2 * pp + half
                        sch.op("t", lambda eng, o=pa_t, half=half, i=i, k=kst_t, q=qst_t, sbk=sbk:
                               eng.matmul(o[:, 512 * half:512 * (half + 1)], k[:, 128 * i:128 * (i + 1)],
                                          q[:, 512 * sbk:512 * (sbk + 1)],
                                          start=True, stop=True, skip_group_check=True),
                               reads=[kst_buf, qst_buf], writes=[pa_buf])
                    r0 = 2 * pp - 4 * sbk
                    if r0 >= 0:
                        for half in (0, 1):
                            r = r0 + half
                            c0 = 512 * half + 128 * r
                            sch.op("s", lambda eng, o=Pp_t, i2=pa_t, c0=c0, c1=512 * (half + 1):
                                   eng.activation(o[:, c0:c1], i2[:, c0:c1], AF.Exp, scale=0.125),
                                   reads=[pa_buf], writes=[Pp_buf])
                            sch.op("v", lambda eng, P=Pp_t, c0=c0:
                                   eng.tensor_tensor(P[:, c0:c0 + 128], P[:, c0:c0 + 128], tri[:], ALU.mult),
                                   reads=[Pp_buf, tri_b], writes=[Pp_buf])
                    else:
                        sch.op("s", lambda eng, o=Pp_t, i2=pa_t:
                               eng.activation(o[:], i2[:], AF.Exp, scale=0.125),
                               reads=[pa_buf], writes=[Pp_buf])
                if pp >= 1:
                    for h2 in (0, 1):
                        emit_pv_pair(pp - 1, h2)
            for h2 in (0, 1):
                emit_pv_pair(npair - 1, h2)
            for h2 in (0, 1):
                pv_t, pv_buf = pvs[h2]
                rc_t, rc_buf = rcp[2 * (sbk % 2) + h2]
                sch.op("v", lambda eng, o=rc_t, i=pv_t:
                       eng.reciprocal(o[:], i[:].rearrange("p (j e) -> p j e", e=65)[:, :, 64]),
                       reads=[pv_buf], writes=[rc_buf])
                h = 2 * u + h2
                for jj in range(4):
                    tau = 4 * sbk + jj
                    sch.op("v", lambda eng, i=pv_t, jj=jj, tau=tau, h=h, rc=rc_t:
                           eng.tensor_scalar(aot[:, tau, 64 * h:64 * (h + 1)],
                                             i[:, 65 * jj:65 * jj + 64], rc[:, jj:jj + 1], None, ALU.mult),
                           reads=[pv_buf, rc_buf], writes=[aot_b])

    sch.drain_dmas()
    sch.emit_block()
    p2.close()

    p3 = ExitStack()
    af = [sb("af%d" % i, [128, 1024], BF16, p3) for i in range(2)]
    xin = [sb("xin%d" % i, [128, D], F32, p3) for i in range(2)]
    yv = [sb("y%d" % i, [128, D], F32, p3) for i in range(2)]
    x1t_ = [sb("x1t%d" % i, [128, D], F32, p3) for i in range(2)]
    x1bf = [sb("x1bf%d" % i, [128, D], BF16, p3) for i in range(2)]
    stats = [sb("stats%d" % i, [128, 18], F32, p3) for i in range(2)]
    atp = [ps("atp%d" % i, [128, 1024], BF16, p3) for i in range(2)]
    wop = [ps("wop%d" % i, [128, 1024], F32, p3) for i in range(2)]
    gcp, gcp_b = ps("gcp", [128, 8], F32, p3)

    def emit_gcsum(g, x1b_t, x1b_buf):
        for c in range(8):
            sch.op("t", lambda eng, g=g, c=c, xb=x1b_t:
                   eng.matmul(gcp[:, c:c + 1], xb[:, 128 * c:128 * (c + 1)], onescol[:],
                              start=(g == 0 and c == 0), stop=(g == NT - 1 and c == 7), skip_group_check=True),
                   reads=[x1b_buf, onescol_b], writes=[gcp_b])

    for g in range(NT):
        af_t, af_buf = af[g % 2]
        xin_t, xin_buf = xin[g % 2]
        y_t, y_buf = yv[g % 2]
        x1_t, x1_buf = x1t_[g % 2]
        x1b_t, x1b_buf = x1bf[g % 2]
        st_t, st_buf = stats[g % 2]
        atp_t, atp_buf = atp[g % 2]
        wop_t, wop_buf = wop[g % 2]
        load("q", xin_t[:], x_d[128 * g:128 * (g + 1), :], [xd_b], [xin_buf])
        for c in range(8):
            sch.op("t", lambda eng, o=atp_t, g=g, c=c:
                   eng.transpose(o[:, 128 * c:128 * (c + 1)], aot[:, g, 128 * c:128 * (c + 1)], identb[:]),
                   reads=[aot_b, identb_b], writes=[atp_buf])
        sch.op("v", lambda eng, o=af_t, i=atp_t: eng.tensor_copy(o[:], i[:]),
               reads=[atp_buf], writes=[af_buf])
        for n2 in range(2):
            for p in range(8):
                sch.op("t", lambda eng, o=wop_t, n2=n2, p=p, a=af_t:
                       eng.matmul(o[:, 512 * n2:512 * (n2 + 1)], a[:, 128 * p:128 * (p + 1)],
                                  wo[:, p, 512 * n2:512 * (n2 + 1)],
                                  start=(p == 0), stop=(p == 7), skip_group_check=True),
                       reads=[af_buf, wo_b], writes=[wop_buf])
        if g >= 1:
            emit_gcsum(g - 1, x1bf[(g - 1) % 2][0], x1bf[(g - 1) % 2][1])
        sch.op("v", lambda eng, o=y_t, i0=xin_t, i1=wop_t:
               eng.tensor_tensor(o[:], i0[:], i1[:], ALU.add),
               reads=[xin_buf, wop_buf], writes=[y_buf])
        sch.op("v", lambda eng, i=y_t, st=st_t: eng.bn_stats(st[:, 0:6], i[:, 0:512]),
               reads=[y_buf], writes=[st_buf])
        sch.op("v", lambda eng, i=y_t, st=st_t: eng.bn_stats(st[:, 6:12], i[:, 512:1024]),
               reads=[y_buf], writes=[st_buf])
        sch.op("v", lambda eng, st=st_t: eng.bn_aggr(st[:, 12:14], st[:, 0:12]),
               reads=[st_buf], writes=[st_buf])
        sch.op("v", lambda eng, st=st_t: eng.tensor_scalar(st[:, 14:15], st[:, 12:13], -1.0, None, ALU.mult),
               reads=[st_buf], writes=[st_buf])
        sch.op("s", lambda eng, st=st_t: eng.activation(st[:, 15:16], st[:, 13:14], AF.Sqrt, bias=epsc[:]),
               reads=[st_buf, epsc_b], writes=[st_buf])
        sch.op("v", lambda eng, st=st_t: eng.reciprocal(st[:, 16:17], st[:, 15:16]),
               reads=[st_buf], writes=[st_buf])
        sch.op("v", lambda eng, o=x1_t, i=y_t, st=st_t:
               eng.tensor_scalar(o[:], i[:], st[:, 14:15], st[:, 16:17], ALU.add, ALU.mult),
               reads=[y_buf, st_buf], writes=[x1_buf])
        sch.op("s", lambda eng, o=x1b_t, i=x1_t: eng.copy(o[:], i[:]),
               reads=[x1_buf], writes=[x1b_buf])
        load("q", x1_d[128 * g:128 * (g + 1), :], x1_t[:], [x1_buf], [x1_b])


    emit_gcsum(NT - 1, x1bf[(NT - 1) % 2][0], x1bf[(NT - 1) % 2][1])
    sch.op("v", lambda eng: eng.tensor_scalar(gcf[:], gcp[:], 1.0 / S, None, ALU.mult),
           reads=[gcp_b], writes=[gcf_b])
    sch.drain_dmas()
    sch.emit_block()
    p3.close()
    p23.close()

    p4w = ExitStack()
    cw1, cw1_b = sb("cw1", [128, 8, D], BF16, p4w)
    cw2, cw2_b = sb("cw2", [128, 8, 512], BF16, p4w)
    cw3, cw3_b = sb("cw3", [128, 4, 256], BF16, p4w)
    cw4, cw4_b = sb("cw4", [128, 2], BF16, p4w)
    gw1, gw1_b = sb("gw1", [128, 8, 512], BF16, p4w)
    gw2, gw2_b = sb("gw2", [128, 4, 256], BF16, p4w)
    gw3, gw3_b = sb("gw3", [128, 2, 128], BF16, p4w)
    load("p", gw1[:], gw1_d.rearrange("(c p) f -> p c f", p=128), [], [gw1_b])
    load("p", gw2[:], gw2_d.rearrange("(c p) f -> p c f", p=128), [], [gw2_b])
    load("p", gw3[:], gw3_d.rearrange("(c p) f -> p c f", p=128), [], [gw3_b])
    load("p", cw1[:], cw1_d.rearrange("(c p) f -> p c f", p=128), [], [cw1_b])
    load("p", cw2[:], cw2_d.rearrange("(c p) f -> p c f", p=128), [], [cw2_b])
    load("p", cw3[:], cw3_d.rearrange("(c p) f -> p c f", p=128), [], [cw3_b])
    load("p", cw4[:], cw4_d[:], [], [cw4_b])

    p4 = ExitStack()
    gcfb, gcfb_b = sb("gcfb", [128, 8], BF16, p4)
    h1g, h1g_b = sb("h1g", [128, 4], BF16, p4)
    h2g, h2g_b = sb("h2g", [128, 2], BF16, p4)
    gfv, gfv_b = sb("gfv", [128, 4], F32, p4)
    x1l = [sb("x1l%d" % i, [128, 4, D], F32, p4) for i in range(2)]
    quarters = [sb("qq%d" % i, [128, 256], F32, p4) for i in range(36)]
    ofs = [sb("of%d" % i, [128, 8, 512], BF16, p4) for i in range(2)]
    h1s = [sb("h1_%d" % i, [128, 8, 512], BF16, p4) for i in range(2)]
    h2s = [sb("h2_%d" % i, [128, 4, 512], BF16, p4) for i in range(2)]
    h3s = [sb("h3_%d" % i, [128, 2, 512], BF16, p4) for i in range(2)]
    tfcs = [sb("tfc%d" % i, [128, 8], F32, p4) for i in range(2)]
    yf = [sb("yf%d" % i, [128, D], F32, p4) for i in range(4)]
    ot = [sb("ot%d" % i, [128, D], F32, p4) for i in range(2)]
    st4 = [sb("st4_%d" % i, [128, 12], F32, p4) for i in range(2)]
    fin, fin_b = sb("fin", [128, 20], F32, p4)
    ofp = [ps("ofp%d" % i, [128, 512], F32, p4) for i in range(4)]
    cnp = [ps("cnp%d" % i, [128, 512], F32, p4) for i in range(2)]
    z4ps = [ps("z4p%d" % i, [128, 8], F32, p4) for i in range(2)]

    z4p, z4p_b = z4ps[0]
    sch.op("s", lambda eng: eng.copy(gcfb[:], gcf[:]), reads=[gcf_b], writes=[gcfb_b])
    for fo in range(4):
        for p in range(8):
            sch.op("t", lambda eng, fo=fo, p=p:
                   eng.matmul(z4p[:, fo:fo + 1], gw1[:, p, 128 * fo:128 * (fo + 1)], gcfb[:, p:p + 1],
                              start=(fo == 0 and p == 0), stop=(fo == 3 and p == 7), skip_group_check=True),
                   reads=[gw1_b, gcfb_b], writes=[z4p_b])
    for fo in range(4):
        sch.op("s", lambda eng, fo=fo:
               eng.activation(h1g[:, fo:fo + 1], z4p[:, fo:fo + 1], AF.Gelu, bias=g1s[:, fo:fo + 1]),
               reads=[z4p_b, g1s_b], writes=[h1g_b])
    for fo in range(2):
        for p in range(4):
            sch.op("t", lambda eng, fo=fo, p=p:
                   eng.matmul(z4p[:, 4 + fo:5 + fo], gw2[:, p, 128 * fo:128 * (fo + 1)], h1g[:, p:p + 1],
                              start=(fo == 0 and p == 0), stop=(fo == 1 and p == 3), skip_group_check=True),
                   reads=[gw2_b, h1g_b], writes=[z4p_b])
    for fo in range(2):
        sch.op("s", lambda eng, fo=fo:
               eng.activation(h2g[:, fo:fo + 1], z4p[:, 4 + fo:5 + fo], AF.Gelu, bias=g2s[:, fo:fo + 1]),
               reads=[z4p_b, g2s_b], writes=[h2g_b])
    for p in range(2):
        sch.op("t", lambda eng, p=p:
               eng.matmul(z4p[:, 6:7], gw3[:, p, :], h2g[:, p:p + 1],
                          start=(p == 0), stop=(p == 1), skip_group_check=True),
               reads=[gw3_b, h2g_b], writes=[z4p_b])
    sch.op("v", lambda eng: eng.tensor_scalar(gfv[:, 1:2], g3s[:], 0.5, None, ALU.mult),
           reads=[g3s_b], writes=[gfv_b])
    sch.op("s", lambda eng: eng.activation(gfv[:, 0:1], z4p[:, 6:7], AF.Tanh, bias=gfv[:, 1:2], scale=0.5),
           reads=[z4p_b, gfv_b], writes=[gfv_b])
    sch.op("v", lambda eng: eng.tensor_scalar(gfv[:, 1:2], gfv[:, 0:1], 0.7, -0.3, ALU.mult, ALU.add),
           reads=[gfv_b], writes=[gfv_b])
    sch.op("v", lambda eng: eng.tensor_scalar(gfv[:, 2:3], gfv[:, 1:2], ars[:], 0.1, ALU.mult, ALU.add),
           reads=[gfv_b, ars_b], writes=[gfv_b])
    sch.op("v", lambda eng: eng.tensor_scalar(gfv[:, 3:4], ars[:], 0.3, None, ALU.mult),
           reads=[ars_b], writes=[gfv_b])
    sch.op("v", lambda eng: eng.tensor_tensor(gfv[:, 2:3], gfv[:, 2:3], gfv[:, 3:4], ALU.add),
           reads=[gfv_b], writes=[gfv_b])
    b4h, b4h_b = sb("b4h", [128, 1], F32, p4)
    sch.op("v", lambda eng: eng.tensor_scalar(b4h[:], b4s[:], 0.5, None, ALU.mult),
           reads=[b4s_b], writes=[b4h_b])

    x1_src = x1_d.rearrange("(b t4 p) e -> b p t4 e", p=128, t4=4)
    qpool = [quarters[18 * par:18 * par + 18] for par in range(2)]
    qbs = {}

    def stage_block(b):
        par = b % 2
        x1l_t, x1l_buf = x1l[par]
        load("q", x1l_t[:], x1_src[b], [x1_b], [x1l_buf])
        qb = [[None] * 4 for _ in range(4)]
        pool = qpool[par]
        idx = 0
        for tau in range(4):
            for qt in range(4):
                qb[tau][qt] = pool[idx]
                idx += 1
        for tau in range(4):
            for qt in range(4):
                t, bfq = qb[tau][qt]
                sch.op("s", lambda eng, t=t, tau=tau, qt=qt, x=x1l_t:
                       eng.copy(t[:], x[:, tau, 256 * qt:256 * (qt + 1)]),
                       reads=[x1l_buf], writes=[bfq])
        qbs[b] = (qb, [pool[16], pool[17]])

    stage_block(0)
    for b in range(NB):
        par = b % 2
        x1l_t, x1l_buf = x1l[par]
        qb, spares = qbs.pop(b)
        of, of_b = ofs[par]
        h1, h1_b = h1s[par]
        h2, h2_b = h2s[par]
        h3, h3_b = h3s[par]
        tfc, tfc_b = tfcs[par]
        z4p, z4p_b = z4ps[par]
        for cyc in range(3):
            for c in range(8):
                ofp_t, ofp_buf = ofp[c % 4]
                qt, off = c // 2, 128 * (c % 2)
                for tau in range(4):
                    t, bfq = qb[tau][qt]
                    sch.op("t", lambda eng, o=ofp_t, t=t, tau=tau, off=off:
                           eng.transpose(o[:, 128 * tau:128 * (tau + 1)], t[:, off:off + 128], identf[:]),
                           reads=[bfq, identf_b], writes=[ofp_buf])
                sch.op("v", lambda eng, c=c, s=ofp_t, of=of: eng.tensor_copy(of[:, c, :], s[:]),
                       reads=[ofp_buf], writes=[of_b])
            for fo in range(8):
                cn_t, cn_buf = cnp[fo % 2]
                for p in range(8):
                    sch.op("t", lambda eng, o=cn_t, fo=fo, p=p, of=of:
                           eng.matmul(o[:], cw1[:, p, 128 * fo:128 * (fo + 1)], of[:, p, :],
                                      start=(p == 0), stop=(p == 7), skip_group_check=True),
                           reads=[cw1_b, of_b], writes=[cn_buf])
                sch.op("s", lambda eng, o=h1, fo=fo, i=cn_t:
                       eng.activation(o[:, fo, :], i[:], AF.Gelu, bias=b1s[:, fo:fo + 1]),
                       reads=[cn_buf, b1s_b], writes=[h1_b])
            for fo in range(4):
                cn_t, cn_buf = cnp[fo % 2]
                for p in range(8):
                    sch.op("t", lambda eng, o=cn_t, fo=fo, p=p, h1=h1:
                           eng.matmul(o[:], cw2[:, p, 128 * fo:128 * (fo + 1)], h1[:, p, :],
                                      start=(p == 0), stop=(p == 7), skip_group_check=True),
                           reads=[cw2_b, h1_b], writes=[cn_buf])
                sch.op("s", lambda eng, o=h2, fo=fo, i=cn_t:
                       eng.activation(o[:, fo, :], i[:], AF.Gelu, bias=b2s[:, fo:fo + 1]),
                       reads=[cn_buf, b2s_b], writes=[h2_b])
            for fo in range(2):
                cn_t, cn_buf = cnp[fo % 2]
                for p in range(4):
                    sch.op("t", lambda eng, o=cn_t, fo=fo, p=p, h2=h2:
                           eng.matmul(o[:], cw3[:, p, 128 * fo:128 * (fo + 1)], h2[:, p, :],
                                      start=(p == 0), stop=(p == 3), skip_group_check=True),
                           reads=[cw3_b, h2_b], writes=[cn_buf])
                sch.op("s", lambda eng, o=h3, fo=fo, i=cn_t:
                       eng.activation(o[:, fo, :], i[:], AF.Gelu, bias=b3s[:, fo:fo + 1]),
                       reads=[cn_buf, b3s_b], writes=[h3_b])
            for tau in range(4):
                for p in range(2):
                    sch.op("t", lambda eng, tau=tau, p=p, h3=h3, z=z4p:
                           eng.matmul(z[:, tau:tau + 1], h3[:, p, 128 * tau:128 * (tau + 1)], cw4[:, p:p + 1],
                                      start=(tau == 0 and p == 0), stop=(tau == 3 and p == 1), skip_group_check=True),
                           reads=[h3_b, cw4_b], writes=[z4p_b])
            sch.op("s", lambda eng, tfc=tfc, z=z4p: eng.activation(tfc[:, 0:4], z[:, 0:4], AF.Tanh,
                                                                   bias=b4h[:], scale=0.5),
                   reads=[z4p_b, b4h_b], writes=[tfc_b])
            sch.op("v", lambda eng, tfc=tfc: eng.tensor_scalar(tfc[:, 0:4], tfc[:, 0:4], gfv[:, 3:4], gfv[:, 2:3],
                                                               ALU.mult, ALU.add),
                   reads=[tfc_b, gfv_b], writes=[tfc_b])
            sch.op("v", lambda eng, tfc=tfc: eng.tensor_scalar(tfc[:, 4:8], tfc[:, 0:4], -1.0, None, ALU.mult),
                   reads=[tfc_b], writes=[tfc_b])
            for tau in range(4):
                for (qa, qc, sgn) in ((0, 2, +1), (1, 3, -1)):
                    ta, ba = qb[tau][qa]
                    tc_, bc = qb[tau][qc]
                    s1t, s1b = spares[0]
                    s2t, s2b = spares[1]
                    ca = tfc[:, (tau if sgn > 0 else 4 + tau):(tau if sgn > 0 else 4 + tau) + 1]
                    cb = tfc[:, (4 + tau if sgn > 0 else tau):(4 + tau if sgn > 0 else tau) + 1]
                    sch.op("v", lambda eng, o=s1t, i0=tc_, sc=ca, i1=ta:
                           eng.scalar_tensor_tensor(o[:], i0[:], sc, i1[:], ALU.mult, ALU.add),
                           reads=[bc, ba, tfc_b], writes=[s1b])
                    sch.op("v", lambda eng, o=s2t, i0=ta, sc=cb, i1=tc_:
                           eng.scalar_tensor_tensor(o[:], i0[:], sc, i1[:], ALU.mult, ALU.add),
                           reads=[ba, bc, tfc_b], writes=[s2b])
                    qb[tau][qa] = spares[0]
                    qb[tau][qc] = spares[1]
                    spares = [(ta, ba), (tc_, bc)]
        if b + 1 < NB:
            stage_block(b + 1)
        for tau in range(4):
            y_t, y_buf = yf[tau]
            st_t, st_buf = st4[tau % 2]
            for qt in range(4):
                t, bfq = qb[tau][qt]
                sch.op("v", lambda eng, o=y_t, qt=qt, t=t, x=x1l_t, tau=tau:
                       eng.tensor_tensor(o[:, 256 * qt:256 * (qt + 1)], t[:], x[:, tau, 256 * qt:256 * (qt + 1)],
                                         ALU.add),
                       reads=[bfq, x1l_buf], writes=[y_buf])
            sch.op("v", lambda eng, i=y_t, st=st_t: eng.bn_stats(st[:, 0:6], i[:, 0:512]),
                   reads=[y_buf], writes=[st_buf])
            sch.op("v", lambda eng, i=y_t, st=st_t: eng.bn_stats(st[:, 6:12], i[:, 512:1024]),
                   reads=[y_buf], writes=[st_buf])
            sch.op("v", lambda eng, st=st_t, tau=tau: eng.bn_aggr(fin[:, 2 * tau:2 * tau + 2], st[:, 0:12]),
                   reads=[st_buf], writes=[fin_b])
        sch.op("v", lambda eng: eng.tensor_scalar(
            fin[:, 8:12], fin[:].rearrange("p (t k) -> p k t", k=2)[:, 0, 0:4], -1.0, None, ALU.mult),
            reads=[fin_b], writes=[fin_b])
        sch.op("s", lambda eng: eng.activation(
            fin[:, 12:16], fin[:].rearrange("p (t k) -> p k t", k=2)[:, 1, 0:4], AF.Sqrt, bias=epsc[:]),
            reads=[fin_b, epsc_b], writes=[fin_b])
        sch.op("v", lambda eng: eng.reciprocal(fin[:, 16:20], fin[:, 12:16]),
               reads=[fin_b], writes=[fin_b])
        for tau in range(4):
            g = 4 * b + tau
            y_t, y_buf = yf[tau]
            o_t, o_buf = ot[tau % 2]
            sch.op("v", lambda eng, o=o_t, i=y_t, tau=tau:
                   eng.tensor_scalar(o[:], i[:], fin[:, 8 + tau:9 + tau], fin[:, 16 + tau:17 + tau],
                                     ALU.add, ALU.mult),
                   reads=[y_buf, fin_b], writes=[o_buf])
            load("q", out_d[128 * g:128 * (g + 1), :], o_t[:], [o_buf], [out_b])

    sch.drain_dmas()
    sch.emit_block()
    p4.close()
    p4w.close()
    root.close()
    return nc


def _host_constants():
    pos = np.arange(S, dtype=np.float64)
    cos = np.ones((128, S), dtype=np.float64)
    sin = np.zeros((128, S), dtype=np.float64)
    for p in range(128):
        dd = p % 64
        if dd < ROPE:
            j = dd // 2
            invf = 10000.0 ** (-(2.0 * j) / ROPE)
            ang = pos * invf
            cos[p] = np.cos(ang)
            sin[p] = np.sin(ang)
    r128 = np.zeros((128, 128), dtype=np.float32)
    for blk in range(2):
        o = 64 * blk
        for j in range(ROPE // 2):
            r128[o + 2 * j + 1, o + 2 * j] = -1.0
            r128[o + 2 * j, o + 2 * j + 1] = 1.0
    tri = (np.arange(128)[:, None] <= np.arange(128)[None, :]).astype(np.float32)
    return (cos.astype(np.float32), sin.astype(np.float32),
            r128.astype(BF), tri.astype(BF))


_NC_CACHE = [None]


def kernel(L0_attn_hidden_states, L0_attn_w_qkv, L0_attn_w_out,
           L0_ffn_cn_w1, L0_ffn_cn_b1, L0_ffn_cn_w2, L0_ffn_cn_b2,
           L0_ffn_cn_w3, L0_ffn_cn_b3, L0_ffn_cn_w4, L0_ffn_cn_b4,
           L0_ffn_gc_w1, L0_ffn_gc_b1, L0_ffn_gc_w2, L0_ffn_gc_b2,
           L0_ffn_gc_w3, L0_ffn_gc_b3, L0_ffn_adaptive_range):
    x = np.ascontiguousarray(np.asarray(L0_attn_hidden_states, dtype=np.float32))
    w_qkv = np.asarray(L0_attn_w_qkv, dtype=np.float32)
    w_out = np.asarray(L0_attn_w_out, dtype=np.float32)

    perm = np.empty(3 * D, dtype=np.int64)
    for c in range(3 * D):
        kind, r = divmod(c, D)
        h, dd = divmod(r, HD)
        base = (h // 4) * 768 + (h % 4) * 64 + dd
        perm[c] = base + (0 if kind == 0 else (512 if kind == 1 else 256))
    wqkvT = np.ascontiguousarray(w_qkv[perm, :].T)
    woutT = np.ascontiguousarray(w_out.T)

    rep = lambda a: np.ascontiguousarray(np.repeat(np.asarray(a, np.float32).reshape(1, -1), 128, axis=0))
    colmaj = lambda a, k: np.ascontiguousarray(np.asarray(a, np.float32).reshape(k, 128).T)

    cos_f, sin_f, r128, tri = _host_constants()
    in_common = dict(
        wqkvT=wqkvT, woutT=woutT,
        cnw1=np.ascontiguousarray(np.asarray(L0_ffn_cn_w1, np.float32)),
        cnw2=np.ascontiguousarray(np.asarray(L0_ffn_cn_w2, np.float32)),
        cnw3=np.ascontiguousarray(np.asarray(L0_ffn_cn_w3, np.float32)),
        cnw4=colmaj(L0_ffn_cn_w4, 2),
        gcw1=np.ascontiguousarray(np.asarray(L0_ffn_gc_w1, np.float32)),
        gcw2=np.ascontiguousarray(np.asarray(L0_ffn_gc_w2, np.float32)),
        gcw3r=np.ascontiguousarray(np.repeat(np.asarray(L0_ffn_gc_w3, np.float32).reshape(256, 1), 128, axis=1)),
        b1=colmaj(L0_ffn_cn_b1, 8), b2=colmaj(L0_ffn_cn_b2, 4), b3=colmaj(L0_ffn_cn_b3, 2),
        b4=rep(L0_ffn_cn_b4).reshape(128, 1),
        g1=colmaj(L0_ffn_gc_b1, 4), g2=colmaj(L0_ffn_gc_b2, 2),
        g3=rep(L0_ffn_gc_b3).reshape(128, 1),
        ar=rep(L0_ffn_adaptive_range).reshape(128, 1),
        identb=np.eye(128, dtype=BF), identf=np.eye(128, dtype=np.float32),
        tri=tri, onescol=np.ones((128, 1), dtype=BF),
        cosf=cos_f, sinf=sin_f, r128t=r128,
    )
    if _NC_CACHE[0] is None:
        _NC_CACHE[0] = build_nc()
    nc = _NC_CACHE[0]
    in_maps = [dict(in_common, x=np.ascontiguousarray(x[i])) for i in range(8)]
    res = run_bass_kernel_spmd(nc, in_maps, core_ids=list(range(8)))
    return np.stack([np.asarray(r["out"], dtype=np.float32) for r in res.results], axis=0)


if __name__ == "__main__":
    np.random.seed(0)
    import reference as _ref
    inputs = _ref.setup_inputs(0)
    got = kernel(**{k: np.asarray(v) for k, v in inputs.items()})
    want = np.asarray(_ref.reference(**inputs))
    num = float(np.mean((got - want) ** 2))
    den = float(np.mean(want ** 2))
    print("resid-var ratio:", num / den)


# version 17
# speedup vs baseline: 1.1449x; 1.0068x over previous
import numpy as np
import ml_dtypes
from contextlib import ExitStack

import concourse.bass as bass
import concourse.mybir as mybir
from concourse.bass_utils import run_bass_kernel_spmd

F32 = mybir.dt.float32
BF16 = mybir.dt.bfloat16
AF = mybir.ActivationFunctionType
ALU = mybir.AluOpType
AX = mybir.AxisListType

S = 4096
D = 1024
H = 16
HD = 64
ROPE = 32
NB = 8
NT = 32
EPS = 1e-5
BF = ml_dtypes.bfloat16


class Buf:
    __slots__ = ("w", "r")

    def __init__(self):
        self.w = None
        self.r = {}


class Sched:
    ENGNAME = {"t": "tensor", "v": "vector", "s": "scalar", "p": "gpsimd", "q": "sync"}

    NL = 8

    def __init__(self, nc, stack):
        self.nc = nc
        self.semkeys = (["t", "v", "s", "p"]
                        + ["q%d" % i for i in range(self.NL)]
                        + ["p%d" % i for i in range(self.NL)]
                        + ["s%d" % i for i in range(4)])
        self.sems = {k: stack.enter_context(nc.semaphore("sem_" + k)) for k in self.semkeys}
        self.cnt = {k: 0 for k in self.semkeys}
        self.waited = {e: {k: 0 for k in self.semkeys} for e in "tvspq"}
        self.thunks = {e: [] for e in "tvspq"}
        self.dmaidx = {"q": 0, "p": 0, "s": 0}

    def _wait(self, e, semkey, val):
        if val <= 0:
            return
        if e == "t" and semkey == "t":
            return
        if self.waited[e][semkey] >= val:
            return
        self.waited[e][semkey] = val
        sem = self.sems[semkey]
        self.thunks[e].append(lambda eng, s=sem, v=val: eng.wait_ge(s, v))

    def op(self, e, emit, reads=(), writes=(), dma=False):
        for b in reads:
            if b.w is not None:
                self._wait(e, b.w[0], b.w[1])
        for b in writes:
            if b.w is not None:
                self._wait(e, b.w[0], b.w[1])
            for sk, v in b.r.items():
                self._wait(e, sk, v)
        if dma:
            nl = 4 if e == "s" else self.NL
            lane = self.dmaidx[e] % nl
            self.dmaidx[e] += 1
            semkey = "%s%d" % (e, lane)
            inc = 16
            self._wait(e, semkey, self.cnt[semkey])
        else:
            semkey = e
            inc = 1
        self.cnt[semkey] += inc
        tag = (semkey, self.cnt[semkey])
        sem = self.sems[semkey]
        self.thunks[e].append(lambda eng, f=emit, s=sem, i=inc: f(eng).then_inc(s, i))
        for b in writes:
            b.w = tag
            b.r = {}
        for b in reads:
            if b.r.get(tag[0], 0) < tag[1]:
                b.r[tag[0]] = tag[1]
        return tag

    def drain_dmas(self):
        for i in range(self.NL):
            self._wait("q", "q%d" % i, self.cnt["q%d" % i])
            self._wait("q", "p%d" % i, self.cnt["p%d" % i])
        for i in range(4):
            self._wait("q", "s%d" % i, self.cnt["s%d" % i])

    def emit_block(self):
        with self.nc.Block() as blk:
            for e in "tvspq":
                if not self.thunks[e]:
                    continue

                def runner(eng, ths=self.thunks[e]):
                    for th in ths:
                        th(eng)

                getattr(blk, self.ENGNAME[e])(runner)
        self.thunks = {e: [] for e in "tvspq"}


def build_nc():
    nc = bass.Bass("TRN2", target_bir_lowering=False)

    x_d = nc.dram_tensor("x", [S, D], F32, kind="ExternalInput").ap()
    wq_d = nc.dram_tensor("wqkvT", [D, 3 * D], F32, kind="ExternalInput").ap()
    wo_d = nc.dram_tensor("woutT", [D, D], F32, kind="ExternalInput").ap()
    cw1_d = nc.dram_tensor("cnw1", [D, D], F32, kind="ExternalInput").ap()
    cw2_d = nc.dram_tensor("cnw2", [D, 512], F32, kind="ExternalInput").ap()
    cw3_d = nc.dram_tensor("cnw3", [512, 256], F32, kind="ExternalInput").ap()
    cw4_d = nc.dram_tensor("cnw4", [128, 2], F32, kind="ExternalInput").ap()
    gw1_d = nc.dram_tensor("gcw1", [D, 512], F32, kind="ExternalInput").ap()
    gw2_d = nc.dram_tensor("gcw2", [512, 256], F32, kind="ExternalInput").ap()
    gw3_d = nc.dram_tensor("gcw3r", [256, 128], F32, kind="ExternalInput").ap()
    b1_d = nc.dram_tensor("b1", [128, 8], F32, kind="ExternalInput").ap()
    b2_d = nc.dram_tensor("b2", [128, 4], F32, kind="ExternalInput").ap()
    b3_d = nc.dram_tensor("b3", [128, 2], F32, kind="ExternalInput").ap()
    b4_d = nc.dram_tensor("b4", [128, 1], F32, kind="ExternalInput").ap()
    g1_d = nc.dram_tensor("g1", [128, 4], F32, kind="ExternalInput").ap()
    g2_d = nc.dram_tensor("g2", [128, 2], F32, kind="ExternalInput").ap()
    g3_d = nc.dram_tensor("g3", [128, 1], F32, kind="ExternalInput").ap()
    ar_d = nc.dram_tensor("ar", [128, 1], F32, kind="ExternalInput").ap()
    idb_d = nc.dram_tensor("identb", [128, 128], BF16, kind="ExternalInput").ap()
    idf_d = nc.dram_tensor("identf", [128, 128], F32, kind="ExternalInput").ap()
    tri_d = nc.dram_tensor("tri", [128, 128], BF16, kind="ExternalInput").ap()
    ones_d = nc.dram_tensor("onescol", [128, 1], BF16, kind="ExternalInput").ap()
    cos_d = nc.dram_tensor("cosf", [128, S], F32, kind="ExternalInput").ap()
    sin_d = nc.dram_tensor("sinf", [128, S], F32, kind="ExternalInput").ap()
    r128_d = nc.dram_tensor("r128t", [128, 128], BF16, kind="ExternalInput").ap()

    qf_d = nc.dram_tensor("qf", [D, S], BF16, kind="Internal").ap()
    kf_d = nc.dram_tensor("kf", [D, S], BF16, kind="Internal").ap()
    vt_d = nc.dram_tensor("vt", [S, D], BF16, kind="Internal").ap()
    x1_d = nc.dram_tensor("x1", [S, D], F32, kind="Internal").ap()
    out_d = nc.dram_tensor("out", [S, D], F32, kind="ExternalOutput").ap()

    root = ExitStack()
    sch = Sched(nc, root)

    def sb(name, shape, dtype, stk):
        t = stk.enter_context(nc.sbuf_tensor("sb_" + name, shape, dtype))
        return t, Buf()

    def ps(name, shape, dtype, stk):
        t = stk.enter_context(nc.psum_tensor("ps_" + name, shape, dtype))
        return t, Buf()

    qf_b, kf_b, vt_b, x1_b, out_b = Buf(), Buf(), Buf(), Buf(), Buf()
    xd_b = Buf()

    identb, identb_b = sb("identb", [128, 128], BF16, root)
    identf, identf_b = sb("identf", [128, 128], F32, root)
    tri, tri_b = sb("tri", [128, 128], BF16, root)
    onescol, onescol_b = sb("onescol", [128, 1], BF16, root)
    b1s, b1s_b = sb("b1s", [128, 8], F32, root)
    b2s, b2s_b = sb("b2s", [128, 4], F32, root)
    b3s, b3s_b = sb("b3s", [128, 2], F32, root)
    b4s, b4s_b = sb("b4s", [128, 1], F32, root)
    g1s, g1s_b = sb("g1s", [128, 4], F32, root)
    g2s, g2s_b = sb("g2s", [128, 2], F32, root)
    g3s, g3s_b = sb("g3s", [128, 1], F32, root)
    ars, ars_b = sb("ars", [128, 1], F32, root)
    epsc, epsc_b = sb("epsc", [128, 1], F32, root)
    gcf, gcf_b = sb("gcf", [128, 8], F32, root)

    def load(e, dst_ap, src_ap, rbufs, wbufs):
        sch.op(e, lambda eng, d=dst_ap, s=src_ap: eng.dma_start(d, s),
               reads=rbufs, writes=wbufs, dma=True)

    for dst, dstb, src in [
        (identb, identb_b, idb_d), (identf, identf_b, idf_d), (tri, tri_b, tri_d),
        (onescol, onescol_b, ones_d), (b1s, b1s_b, b1_d), (b2s, b2s_b, b2_d),
        (b3s, b3s_b, b3_d), (b4s, b4s_b, b4_d), (g1s, g1s_b, g1_d),
        (g2s, g2s_b, g2_d), (g3s, g3s_b, g3_d), (ars, ars_b, ar_d),
    ]:
        load("q", dst[:], src[:], [], [dstb])
    sch.op("v", lambda eng: eng.memset(epsc[:], EPS), reads=[], writes=[epsc_b])

    p1 = ExitStack()
    wq, wq_b = sb("wq", [128, 8, 3 * D], BF16, p1)
    cosf, cosf_b = sb("cosf", [128, S], F32, p1)
    sinf, sinf_b = sb("sinf", [128, S], F32, p1)
    r128, r128_b = sb("r128", [128, 128], BF16, p1)
    load("q", r128[:], r128_d[:], [], [r128_b])
    load("q", cosf[:], cos_d[:], [], [cosf_b])
    load("q", sinf[:], sin_d[:], [], [sinf_b])
    wq_src = wq_d.rearrange("(c p) f -> p c f", p=128)
    load("p", wq[:, 0:4, :], wq_src[:, 0:4, :], [], [wq_b])
    load("p", wq[:, 4:8, :], wq_src[:, 4:8, :], [], [wq_b])

    xt = [sb("xt%d" % i, [128, D], F32, p1) for i in range(2)]
    xtb = [sb("xtb%d" % i, [128, D], BF16, p1) for i in range(2)]
    xf = [sb("xf%d" % i, [128, 8, 512], BF16, p1) for i in range(2)]
    qkraw, qkraw_b = sb("qkraw", [128, 16, 512], BF16, p1)
    qkraw_bufs = [Buf() for _ in range(16)]
    qkro = [sb("qkro%d" % i, [128, 16, 512], BF16, p1) for i in range(2)]
    vraw = [sb("vraw%d" % i, [128, 512], BF16, p1) for i in range(2)]
    vsb, vsb_b = sb("vsb", [128, 8, 4, 128], BF16, p1)
    tsin = [sb("tsin%d" % i, [128, 512], F32, p1) for i in range(2)]
    tcos = [sb("tcos%d" % i, [128, 512], F32, p1) for i in range(2)]

    trp = [ps("trp%d" % i, [128, 1024], BF16, p1) for i in range(2)]
    qps = [ps("qps%d" % i, [128, 512], F32, p1) for i in range(2)]
    rp = [ps("rp%d" % i, [128, 512], F32, p1) for i in range(2)]
    vtp, vtp_b = ps("vtp", [128, 512], BF16, p1)

    for b in range(NB):
        xfb, xfb_buf = xf[b % 2]
        qkro_t, qkro_buf = qkro[b % 2]
        for tau in range(4):
            g = 4 * b + tau
            xt_t, xt_buf = xt[g % 2]
            xtb_t, xtb_buf = xtb[g % 2]
            trp_t, trp_buf = trp[g % 2]
            load("q", xt_t[:], x_d[128 * g:128 * (g + 1), :], [xd_b], [xt_buf])
            sch.op("s", lambda eng, o=xtb_t, i=xt_t: eng.copy(o[:], i[:]),
                   reads=[xt_buf], writes=[xtb_buf])
            for c in range(8):
                sch.op("t", lambda eng, o=trp_t, i=xtb_t, c=c:
                       eng.transpose(o[:, 128 * c:128 * (c + 1)], i[:, 128 * c:128 * (c + 1)], identb[:]),
                       reads=[xtb_buf, identb_b], writes=[trp_buf])
            dst = xfb[:].rearrange("p c (t4 e) -> p c t4 e", t4=4)[:, :, tau, :]
            src = trp_t[:].rearrange("p (c e) -> p c e", c=8)
            sch.op("v", lambda eng, d=dst, s=src: eng.tensor_copy(d, s),
                   reads=[trp_buf], writes=[xfb_buf])
        for fo in range(24):
            qp_t, qp_buf = qps[fo % 2]
            for p in range(8):
                sch.op("t", lambda eng, o=qp_t, fo=fo, p=p, x=xfb:
                       eng.matmul(o[:], wq[:, p, 128 * fo:128 * (fo + 1)], x[:, p, :],
                                  start=(p == 0), stop=(p == 7), skip_group_check=True),
                       reads=[wq_b, xfb_buf], writes=[qp_buf])
            if fo < 16:
                c = fo
                sch.op("v", lambda eng, c=c, i=qp_t: eng.tensor_copy(qkraw[:, c, :], i[:]),
                       reads=[qp_buf], writes=[qkraw_bufs[c]])
            else:
                c = fo - 16
                vr_t, vr_buf = vraw[c % 2]
                sch.op("v", lambda eng, o=vr_t, i=qp_t: eng.tensor_copy(o[:], i[:]),
                       reads=[qp_buf], writes=[vr_buf])
                for tau in range(4):
                    sch.op("t", lambda eng, i=vr_t, tau=tau:
                           eng.transpose(vtp[:, 128 * tau:128 * (tau + 1)], i[:, 128 * tau:128 * (tau + 1)], identb[:]),
                           reads=[vr_buf, identb_b], writes=[vtp_b])
                sch.op("v", lambda eng, c=c: eng.tensor_copy(
                    vsb[:, c, :, :], vtp[:].rearrange("p (t4 e) -> p t4 e", t4=4)),
                    reads=[vtp_b], writes=[vsb_b])
        for c in range(16):
            rp_t, rp_buf = rp[c % 2]
            ts_t, ts_buf = tsin[c % 2]
            tc_t, tc_buf = tcos[c % 2]
            sch.op("t", lambda eng, o=rp_t, c=c:
                   eng.matmul(o[:], r128[:], qkraw[:, c, :], start=True, stop=True, skip_group_check=True),
                   reads=[r128_b, qkraw_bufs[c]], writes=[rp_buf])
            sch.op("v", lambda eng, o=ts_t, i=rp_t, b=b:
                   eng.tensor_tensor(o[:], i[:], sinf[:, 512 * b:512 * (b + 1)], ALU.mult),
                   reads=[rp_buf, sinf_b], writes=[ts_buf])
            sch.op("p", lambda eng, o=tc_t, c=c, b=b:
                   eng.tensor_tensor(o[:], qkraw[:, c, :], cosf[:, 512 * b:512 * (b + 1)], ALU.mult),
                   reads=[qkraw_bufs[c], cosf_b], writes=[tc_buf])
            sch.op("p", lambda eng, o=qkro_t, c=c, i0=tc_t, i1=ts_t:
                   eng.tensor_tensor(o[:, c, :], i0[:], i1[:], ALU.add),
                   reads=[tc_buf, ts_buf], writes=[qkro_buf])
        qdst = qf_d.rearrange("(c p) s -> p c s", p=128)[:, :, 512 * b:512 * (b + 1)]
        kdst = kf_d.rearrange("(c p) s -> p c s", p=128)[:, :, 512 * b:512 * (b + 1)]
        vdst = vt_d.rearrange("(B t4 p) (c e) -> B p c t4 e", t4=4, p=128, e=128)[b]
        sch.op("p", lambda eng, d=qdst, s=qkro_t: eng.dma_start(d, s[:, 0:8, :]),
               reads=[qkro_buf], writes=[qf_b], dma=True)
        sch.op("p", lambda eng, d=kdst, s=qkro_t: eng.dma_start(d, s[:, 8:16, :]),
               reads=[qkro_buf], writes=[kf_b], dma=True)
        sch.op("p", lambda eng, d=vdst: eng.dma_start(d, vsb[:]),
               reads=[vsb_b], writes=[vt_b], dma=True)

    sch.drain_dmas()
    sch.emit_block()
    p1.close()

    p23 = ExitStack()
    aot, aot_b = sb("aot", [128, NT, D], BF16, p23)
    wo, wo_b = sb("wo", [128, 8, D], BF16, p23)
    load("p", wo[:], wo_d.rearrange("(c p) f -> p c f", p=128), [], [wo_b])

    p2 = ExitStack()
    kst = [sb("kst%d" % i, [128, S], BF16, p2) for i in range(4)]
    qst = [sb("qst%d" % i, [128, S], BF16, p2) for i in range(4)]
    vst = [sb("vst%d" % i, [128, NT, 130], BF16, p2) for i in range(2)]
    Ppair = [[sb("Pp%d_%d" % (h2, i), [128, 1024], BF16, p2) for i in range(3)] for h2 in range(2)]
    rcp = [sb("rcp%d" % i, [128, 4], F32, p2) for i in range(4)]
    pairps = [ps("pair%d" % i, [128, 1024], F32, p2) for i in range(2)]
    pvps = [[ps("pv%d_%d" % (h2, i), [128, 260], F32, p2) for i in range(2)] for h2 in range(2)]

    for i in range(4):
        t, bfk = kst[i]
        sch.op("v", lambda eng, t=t: eng.memset(t[64:128, :], 0.0), reads=[], writes=[bfk])
        t, bfq2 = qst[i]
        sch.op("v", lambda eng, t=t: eng.memset(t[64:128, :], 0.0), reads=[], writes=[bfq2])
    for i in range(2):
        t, bfv = vst[i]
        sch.op("v", lambda eng, t=t: eng.memset(t[:].rearrange("p i (h2 e) -> p i h2 e", e=65)[:, :, :, 64], 1.0),
               reads=[], writes=[bfv])

    vt_src = vt_d.rearrange("(i p) e -> p i e", p=128)
    for u in range(H // 2):
        ub = u % 2
        ks = [kst[2 * ub], kst[2 * ub + 1]]
        qs = [qst[2 * ub], qst[2 * ub + 1]]
        vst_t, vst_buf = vst[ub]
        for h2 in (0, 1):
            h = 2 * u + h2
            load("q", ks[h2][0][0:64, :], kf_d[64 * h:64 * (h + 1), :], [kf_b], [ks[h2][1]])
            load("q", qs[h2][0][0:64, :], qf_d[64 * h:64 * (h + 1), :], [qf_b], [qs[h2][1]])
        vv = vst_t[:].rearrange("p i (h2 e) -> p i h2 e", e=65)
        load("q", vv[:, :, 0, 0:64], vt_src[:, :, 128 * u:128 * u + 64], [vt_b], [vst_buf])
        load("q", vv[:, :, 1, 0:64], vt_src[:, :, 128 * u + 64:128 * u + 128], [vt_b], [vst_buf])
        for sbk in range(NB):
            nI = 4 * sbk + 4
            npair = nI // 2
            pvs = [pvps[0][sbk % 2], pvps[1][sbk % 2]]
            first = [True, True]

            def emit_pv_pair(pp, h2, sbk=sbk, nI=nI, pvs=pvs, first=first,
                             vst_t=vst_t, vst_buf=vst_buf):
                Pp_t, Pp_buf = Ppair[h2][pp % 3]
                pv_t, pv_buf = pvs[h2]
                for half in (0, 1):
                    i = 2 * pp + half
                    for jj in range(4):
                        if i > 4 * sbk + jj:
                            continue
                        st0 = first[h2]
                        first[h2] = False
                        last = (i == nI - 1) and (jj == 3)
                        sch.op("t", lambda eng, o=pv_t, P=Pp_t, half=half, jj=jj, i=i, h2=h2,
                               st0=st0, last=last, v=vst_t:
                               eng.matmul(o[:, 65 * jj:65 * jj + 65],
                                          P[:, 512 * half + 128 * jj:512 * half + 128 * (jj + 1)],
                                          v[:, i, 65 * h2:65 * h2 + 65],
                                          start=st0, stop=last, skip_group_check=True),
                               reads=[Pp_buf, vst_buf], writes=[pv_buf])

            for pp in range(npair):
                for h2 in (0, 1):
                    pa_t, pa_buf = pairps[h2]
                    Pp_t, Pp_buf = Ppair[h2][pp % 3]
                    kst_t, kst_buf = ks[h2]
                    qst_t, qst_buf = qs[h2]
                    for half in (0, 1):
                        i = 2 * pp + half
                        sch.op("t", lambda eng, o=pa_t, half=half, i=i, k=kst_t, q=qst_t, sbk=sbk:
                               eng.matmul(o[:, 512 * half:512 * (half + 1)], k[:, 128 * i:128 * (i + 1)],
                                          q[:, 512 * sbk:512 * (sbk + 1)],
                                          start=True, stop=True, skip_group_check=True),
                               reads=[kst_buf, qst_buf], writes=[pa_buf])
                    r0 = 2 * pp - 4 * sbk
                    if r0 >= 0:
                        for half in (0, 1):
                            r = r0 + half
                            c0 = 512 * half + 128 * r
                            sch.op("s", lambda eng, o=Pp_t, i2=pa_t, c0=c0, c1=512 * (half + 1):
                                   eng.activation(o[:, c0:c1], i2[:, c0:c1], AF.Exp, scale=0.125),
                                   reads=[pa_buf], writes=[Pp_buf])
                            sch.op("v", lambda eng, P=Pp_t, c0=c0:
                                   eng.tensor_tensor(P[:, c0:c0 + 128], P[:, c0:c0 + 128], tri[:], ALU.mult),
                                   reads=[Pp_buf, tri_b], writes=[Pp_buf])
                    else:
                        sch.op("s", lambda eng, o=Pp_t, i2=pa_t:
                               eng.activation(o[:], i2[:], AF.Exp, scale=0.125),
                               reads=[pa_buf], writes=[Pp_buf])
                if pp >= 2:
                    for h2 in (0, 1):
                        emit_pv_pair(pp - 2, h2)
            if npair >= 2:
                for h2 in (0, 1):
                    emit_pv_pair(npair - 2, h2)
            for h2 in (0, 1):
                emit_pv_pair(npair - 1, h2)
            for h2 in (0, 1):
                pv_t, pv_buf = pvs[h2]
                rc_t, rc_buf = rcp[2 * (sbk % 2) + h2]
                sch.op("v", lambda eng, o=rc_t, i=pv_t:
                       eng.reciprocal(o[:], i[:].rearrange("p (j e) -> p j e", e=65)[:, :, 64]),
                       reads=[pv_buf], writes=[rc_buf])
                h = 2 * u + h2
                for jj in range(4):
                    tau = 4 * sbk + jj
                    sch.op("v", lambda eng, i=pv_t, jj=jj, tau=tau, h=h, rc=rc_t:
                           eng.tensor_scalar(aot[:, tau, 64 * h:64 * (h + 1)],
                                             i[:, 65 * jj:65 * jj + 64], rc[:, jj:jj + 1], None, ALU.mult),
                           reads=[pv_buf, rc_buf], writes=[aot_b])

    sch.drain_dmas()
    sch.emit_block()
    p2.close()

    p3 = ExitStack()
    af = [sb("af%d" % i, [128, 1024], BF16, p3) for i in range(2)]
    xin = [sb("xin%d" % i, [128, D], F32, p3) for i in range(2)]
    yv = [sb("y%d" % i, [128, D], F32, p3) for i in range(2)]
    x1t_ = [sb("x1t%d" % i, [128, D], F32, p3) for i in range(2)]
    x1bf = [sb("x1bf%d" % i, [128, D], BF16, p3) for i in range(2)]
    stats = [sb("stats%d" % i, [128, 18], F32, p3) for i in range(2)]
    atp = [ps("atp%d" % i, [128, 1024], BF16, p3) for i in range(2)]
    wop = [ps("wop%d" % i, [128, 1024], F32, p3) for i in range(2)]
    gcp, gcp_b = ps("gcp", [128, 8], F32, p3)

    def emit_gcsum(g, x1b_t, x1b_buf):
        for c in range(8):
            sch.op("t", lambda eng, g=g, c=c, xb=x1b_t:
                   eng.matmul(gcp[:, c:c + 1], xb[:, 128 * c:128 * (c + 1)], onescol[:],
                              start=(g == 0 and c == 0), stop=(g == NT - 1 and c == 7), skip_group_check=True),
                   reads=[x1b_buf, onescol_b], writes=[gcp_b])

    for g in range(NT):
        af_t, af_buf = af[g % 2]
        xin_t, xin_buf = xin[g % 2]
        y_t, y_buf = yv[g % 2]
        x1_t, x1_buf = x1t_[g % 2]
        x1b_t, x1b_buf = x1bf[g % 2]
        st_t, st_buf = stats[g % 2]
        atp_t, atp_buf = atp[g % 2]
        wop_t, wop_buf = wop[g % 2]
        load("q", xin_t[:], x_d[128 * g:128 * (g + 1), :], [xd_b], [xin_buf])
        for c in range(8):
            sch.op("t", lambda eng, o=atp_t, g=g, c=c:
                   eng.transpose(o[:, 128 * c:128 * (c + 1)], aot[:, g, 128 * c:128 * (c + 1)], identb[:]),
                   reads=[aot_b, identb_b], writes=[atp_buf])
        sch.op("v", lambda eng, o=af_t, i=atp_t: eng.tensor_copy(o[:], i[:]),
               reads=[atp_buf], writes=[af_buf])
        for n2 in range(2):
            for p in range(8):
                sch.op("t", lambda eng, o=wop_t, n2=n2, p=p, a=af_t:
                       eng.matmul(o[:, 512 * n2:512 * (n2 + 1)], a[:, 128 * p:128 * (p + 1)],
                                  wo[:, p, 512 * n2:512 * (n2 + 1)],
                                  start=(p == 0), stop=(p == 7), skip_group_check=True),
                       reads=[af_buf, wo_b], writes=[wop_buf])
        if g >= 1:
            emit_gcsum(g - 1, x1bf[(g - 1) % 2][0], x1bf[(g - 1) % 2][1])
        sch.op("v", lambda eng, o=y_t, i0=xin_t, i1=wop_t:
               eng.tensor_tensor(o[:], i0[:], i1[:], ALU.add),
               reads=[xin_buf, wop_buf], writes=[y_buf])
        sch.op("v", lambda eng, i=y_t, st=st_t: eng.bn_stats(st[:, 0:6], i[:, 0:512]),
               reads=[y_buf], writes=[st_buf])
        sch.op("v", lambda eng, i=y_t, st=st_t: eng.bn_stats(st[:, 6:12], i[:, 512:1024]),
               reads=[y_buf], writes=[st_buf])
        sch.op("v", lambda eng, st=st_t: eng.bn_aggr(st[:, 12:14], st[:, 0:12]),
               reads=[st_buf], writes=[st_buf])
        sch.op("v", lambda eng, st=st_t: eng.tensor_scalar(st[:, 14:15], st[:, 12:13], -1.0, None, ALU.mult),
               reads=[st_buf], writes=[st_buf])
        sch.op("s", lambda eng, st=st_t: eng.activation(st[:, 15:16], st[:, 13:14], AF.Sqrt, bias=epsc[:]),
               reads=[st_buf, epsc_b], writes=[st_buf])
        sch.op("v", lambda eng, st=st_t: eng.reciprocal(st[:, 16:17], st[:, 15:16]),
               reads=[st_buf], writes=[st_buf])
        sch.op("v", lambda eng, o=x1_t, i=y_t, st=st_t:
               eng.tensor_scalar(o[:], i[:], st[:, 14:15], st[:, 16:17], ALU.add, ALU.mult),
               reads=[y_buf, st_buf], writes=[x1_buf])
        sch.op("s", lambda eng, o=x1b_t, i=x1_t: eng.copy(o[:], i[:]),
               reads=[x1_buf], writes=[x1b_buf])
        load("q", x1_d[128 * g:128 * (g + 1), :], x1_t[:], [x1_buf], [x1_b])


    emit_gcsum(NT - 1, x1bf[(NT - 1) % 2][0], x1bf[(NT - 1) % 2][1])
    sch.op("v", lambda eng: eng.tensor_scalar(gcf[:], gcp[:], 1.0 / S, None, ALU.mult),
           reads=[gcp_b], writes=[gcf_b])
    sch.drain_dmas()
    sch.emit_block()
    p3.close()
    p23.close()

    p4w = ExitStack()
    cw1, cw1_b = sb("cw1", [128, 8, D], BF16, p4w)
    cw2, cw2_b = sb("cw2", [128, 8, 512], BF16, p4w)
    cw3, cw3_b = sb("cw3", [128, 4, 256], BF16, p4w)
    cw4, cw4_b = sb("cw4", [128, 2], BF16, p4w)
    gw1, gw1_b = sb("gw1", [128, 8, 512], BF16, p4w)
    gw2, gw2_b = sb("gw2", [128, 4, 256], BF16, p4w)
    gw3, gw3_b = sb("gw3", [128, 2, 128], BF16, p4w)
    load("p", gw1[:], gw1_d.rearrange("(c p) f -> p c f", p=128), [], [gw1_b])
    load("p", gw2[:], gw2_d.rearrange("(c p) f -> p c f", p=128), [], [gw2_b])
    load("p", gw3[:], gw3_d.rearrange("(c p) f -> p c f", p=128), [], [gw3_b])
    load("p", cw1[:], cw1_d.rearrange("(c p) f -> p c f", p=128), [], [cw1_b])
    load("p", cw2[:], cw2_d.rearrange("(c p) f -> p c f", p=128), [], [cw2_b])
    load("p", cw3[:], cw3_d.rearrange("(c p) f -> p c f", p=128), [], [cw3_b])
    load("p", cw4[:], cw4_d[:], [], [cw4_b])

    p4 = ExitStack()
    gcfb, gcfb_b = sb("gcfb", [128, 8], BF16, p4)
    h1g, h1g_b = sb("h1g", [128, 4], BF16, p4)
    h2g, h2g_b = sb("h2g", [128, 2], BF16, p4)
    gfv, gfv_b = sb("gfv", [128, 4], F32, p4)
    x1l = [sb("x1l%d" % i, [128, 4, D], F32, p4) for i in range(2)]
    quarters = [sb("qq%d" % i, [128, 256], F32, p4) for i in range(36)]
    ofs = [sb("of%d" % i, [128, 8, 512], BF16, p4) for i in range(2)]
    h1s = [sb("h1_%d" % i, [128, 8, 512], BF16, p4) for i in range(2)]
    h2s = [sb("h2_%d" % i, [128, 4, 512], BF16, p4) for i in range(2)]
    h3s = [sb("h3_%d" % i, [128, 2, 512], BF16, p4) for i in range(2)]
    tfcs = [sb("tfc%d" % i, [128, 8], F32, p4) for i in range(2)]
    yf = [sb("yf%d" % i, [128, D], F32, p4) for i in range(4)]
    ot = [sb("ot%d" % i, [128, D], F32, p4) for i in range(2)]
    st4 = [sb("st4_%d" % i, [128, 12], F32, p4) for i in range(2)]
    fin, fin_b = sb("fin", [128, 20], F32, p4)
    ofp = [ps("ofp%d" % i, [128, 512], F32, p4) for i in range(4)]
    cnp = [ps("cnp%d" % i, [128, 512], F32, p4) for i in range(2)]
    z4ps = [ps("z4p%d" % i, [128, 8], F32, p4) for i in range(2)]

    z4p, z4p_b = z4ps[0]
    sch.op("s", lambda eng: eng.copy(gcfb[:], gcf[:]), reads=[gcf_b], writes=[gcfb_b])
    for fo in range(4):
        for p in range(8):
            sch.op("t", lambda eng, fo=fo, p=p:
                   eng.matmul(z4p[:, fo:fo + 1], gw1[:, p, 128 * fo:128 * (fo + 1)], gcfb[:, p:p + 1],
                              start=(fo == 0 and p == 0), stop=(fo == 3 and p == 7), skip_group_check=True),
                   reads=[gw1_b, gcfb_b], writes=[z4p_b])
    for fo in range(4):
        sch.op("s", lambda eng, fo=fo:
               eng.activation(h1g[:, fo:fo + 1], z4p[:, fo:fo + 1], AF.Gelu, bias=g1s[:, fo:fo + 1]),
               reads=[z4p_b, g1s_b], writes=[h1g_b])
    for fo in range(2):
        for p in range(4):
            sch.op("t", lambda eng, fo=fo, p=p:
                   eng.matmul(z4p[:, 4 + fo:5 + fo], gw2[:, p, 128 * fo:128 * (fo + 1)], h1g[:, p:p + 1],
                              start=(fo == 0 and p == 0), stop=(fo == 1 and p == 3), skip_group_check=True),
                   reads=[gw2_b, h1g_b], writes=[z4p_b])
    for fo in range(2):
        sch.op("s", lambda eng, fo=fo:
               eng.activation(h2g[:, fo:fo + 1], z4p[:, 4 + fo:5 + fo], AF.Gelu, bias=g2s[:, fo:fo + 1]),
               reads=[z4p_b, g2s_b], writes=[h2g_b])
    for p in range(2):
        sch.op("t", lambda eng, p=p:
               eng.matmul(z4p[:, 6:7], gw3[:, p, :], h2g[:, p:p + 1],
                          start=(p == 0), stop=(p == 1), skip_group_check=True),
               reads=[gw3_b, h2g_b], writes=[z4p_b])
    sch.op("v", lambda eng: eng.tensor_scalar(gfv[:, 1:2], g3s[:], 0.5, None, ALU.mult),
           reads=[g3s_b], writes=[gfv_b])
    sch.op("s", lambda eng: eng.activation(gfv[:, 0:1], z4p[:, 6:7], AF.Tanh, bias=gfv[:, 1:2], scale=0.5),
           reads=[z4p_b, gfv_b], writes=[gfv_b])
    sch.op("v", lambda eng: eng.tensor_scalar(gfv[:, 1:2], gfv[:, 0:1], 0.7, -0.3, ALU.mult, ALU.add),
           reads=[gfv_b], writes=[gfv_b])
    sch.op("v", lambda eng: eng.tensor_scalar(gfv[:, 2:3], gfv[:, 1:2], ars[:], 0.1, ALU.mult, ALU.add),
           reads=[gfv_b, ars_b], writes=[gfv_b])
    sch.op("v", lambda eng: eng.tensor_scalar(gfv[:, 3:4], ars[:], 0.3, None, ALU.mult),
           reads=[ars_b], writes=[gfv_b])
    sch.op("v", lambda eng: eng.tensor_tensor(gfv[:, 2:3], gfv[:, 2:3], gfv[:, 3:4], ALU.add),
           reads=[gfv_b], writes=[gfv_b])
    b4h, b4h_b = sb("b4h", [128, 1], F32, p4)
    sch.op("v", lambda eng: eng.tensor_scalar(b4h[:], b4s[:], 0.5, None, ALU.mult),
           reads=[b4s_b], writes=[b4h_b])

    x1_src = x1_d.rearrange("(b t4 p) e -> b p t4 e", p=128, t4=4)
    qpool = [quarters[18 * par:18 * par + 18] for par in range(2)]
    qbs = {}

    def stage_block(b):
        par = b % 2
        x1l_t, x1l_buf = x1l[par]
        load("q", x1l_t[:], x1_src[b], [x1_b], [x1l_buf])
        qb = [[None] * 4 for _ in range(4)]
        pool = qpool[par]
        idx = 0
        for tau in range(4):
            for qt in range(4):
                qb[tau][qt] = pool[idx]
                idx += 1
        for tau in range(4):
            for qt in range(4):
                t, bfq = qb[tau][qt]
                sch.op("s", lambda eng, t=t, tau=tau, qt=qt, x=x1l_t:
                       eng.copy(t[:], x[:, tau, 256 * qt:256 * (qt + 1)]),
                       reads=[x1l_buf], writes=[bfq])
        qbs[b] = (qb, [pool[16], pool[17]])

    stage_block(0)
    for b in range(NB):
        par = b % 2
        x1l_t, x1l_buf = x1l[par]
        qb, spares = qbs.pop(b)
        of, of_b = ofs[par]
        h1, h1_b = h1s[par]
        h2, h2_b = h2s[par]
        h3, h3_b = h3s[par]
        tfc, tfc_b = tfcs[par]
        z4p, z4p_b = z4ps[par]
        for cyc in range(3):
            for c in range(8):
                ofp_t, ofp_buf = ofp[c % 4]
                qt, off = c // 2, 128 * (c % 2)
                for tau in range(4):
                    t, bfq = qb[tau][qt]
                    sch.op("t", lambda eng, o=ofp_t, t=t, tau=tau, off=off:
                           eng.transpose(o[:, 128 * tau:128 * (tau + 1)], t[:, off:off + 128], identf[:]),
                           reads=[bfq, identf_b], writes=[ofp_buf])
                sch.op("v", lambda eng, c=c, s=ofp_t, of=of: eng.tensor_copy(of[:, c, :], s[:]),
                       reads=[ofp_buf], writes=[of_b])
            for fo in range(8):
                cn_t, cn_buf = cnp[fo % 2]
                for p in range(8):
                    sch.op("t", lambda eng, o=cn_t, fo=fo, p=p, of=of:
                           eng.matmul(o[:], cw1[:, p, 128 * fo:128 * (fo + 1)], of[:, p, :],
                                      start=(p == 0), stop=(p == 7), skip_group_check=True),
                           reads=[cw1_b, of_b], writes=[cn_buf])
                sch.op("s", lambda eng, o=h1, fo=fo, i=cn_t:
                       eng.activation(o[:, fo, :], i[:], AF.Gelu, bias=b1s[:, fo:fo + 1]),
                       reads=[cn_buf, b1s_b], writes=[h1_b])
            for fo in range(4):
                cn_t, cn_buf = cnp[fo % 2]
                for p in range(8):
                    sch.op("t", lambda eng, o=cn_t, fo=fo, p=p, h1=h1:
                           eng.matmul(o[:], cw2[:, p, 128 * fo:128 * (fo + 1)], h1[:, p, :],
                                      start=(p == 0), stop=(p == 7), skip_group_check=True),
                           reads=[cw2_b, h1_b], writes=[cn_buf])
                sch.op("s", lambda eng, o=h2, fo=fo, i=cn_t:
                       eng.activation(o[:, fo, :], i[:], AF.Gelu, bias=b2s[:, fo:fo + 1]),
                       reads=[cn_buf, b2s_b], writes=[h2_b])
            for fo in range(2):
                cn_t, cn_buf = cnp[fo % 2]
                for p in range(4):
                    sch.op("t", lambda eng, o=cn_t, fo=fo, p=p, h2=h2:
                           eng.matmul(o[:], cw3[:, p, 128 * fo:128 * (fo + 1)], h2[:, p, :],
                                      start=(p == 0), stop=(p == 3), skip_group_check=True),
                           reads=[cw3_b, h2_b], writes=[cn_buf])
                sch.op("s", lambda eng, o=h3, fo=fo, i=cn_t:
                       eng.activation(o[:, fo, :], i[:], AF.Gelu, bias=b3s[:, fo:fo + 1]),
                       reads=[cn_buf, b3s_b], writes=[h3_b])
            for tau in range(4):
                for p in range(2):
                    sch.op("t", lambda eng, tau=tau, p=p, h3=h3, z=z4p:
                           eng.matmul(z[:, tau:tau + 1], h3[:, p, 128 * tau:128 * (tau + 1)], cw4[:, p:p + 1],
                                      start=(tau == 0 and p == 0), stop=(tau == 3 and p == 1), skip_group_check=True),
                           reads=[h3_b, cw4_b], writes=[z4p_b])
            sch.op("s", lambda eng, tfc=tfc, z=z4p: eng.activation(tfc[:, 0:4], z[:, 0:4], AF.Tanh,
                                                                   bias=b4h[:], scale=0.5),
                   reads=[z4p_b, b4h_b], writes=[tfc_b])
            sch.op("v", lambda eng, tfc=tfc: eng.tensor_scalar(tfc[:, 0:4], tfc[:, 0:4], gfv[:, 3:4], gfv[:, 2:3],
                                                               ALU.mult, ALU.add),
                   reads=[tfc_b, gfv_b], writes=[tfc_b])
            sch.op("v", lambda eng, tfc=tfc: eng.tensor_scalar(tfc[:, 4:8], tfc[:, 0:4], -1.0, None, ALU.mult),
                   reads=[tfc_b], writes=[tfc_b])
            for tau in range(4):
                for (qa, qc, sgn) in ((0, 2, +1), (1, 3, -1)):
                    ta, ba = qb[tau][qa]
                    tc_, bc = qb[tau][qc]
                    s1t, s1b = spares[0]
                    s2t, s2b = spares[1]
                    ca = tfc[:, (tau if sgn > 0 else 4 + tau):(tau if sgn > 0 else 4 + tau) + 1]
                    cb = tfc[:, (4 + tau if sgn > 0 else tau):(4 + tau if sgn > 0 else tau) + 1]
                    sch.op("v", lambda eng, o=s1t, i0=tc_, sc=ca, i1=ta:
                           eng.scalar_tensor_tensor(o[:], i0[:], sc, i1[:], ALU.mult, ALU.add),
                           reads=[bc, ba, tfc_b], writes=[s1b])
                    sch.op("v", lambda eng, o=s2t, i0=ta, sc=cb, i1=tc_:
                           eng.scalar_tensor_tensor(o[:], i0[:], sc, i1[:], ALU.mult, ALU.add),
                           reads=[ba, bc, tfc_b], writes=[s2b])
                    qb[tau][qa] = spares[0]
                    qb[tau][qc] = spares[1]
                    spares = [(ta, ba), (tc_, bc)]
        if b + 1 < NB:
            stage_block(b + 1)
        for tau in range(4):
            y_t, y_buf = yf[tau]
            st_t, st_buf = st4[tau % 2]
            for qt in range(4):
                t, bfq = qb[tau][qt]
                sch.op("v", lambda eng, o=y_t, qt=qt, t=t, x=x1l_t, tau=tau:
                       eng.tensor_tensor(o[:, 256 * qt:256 * (qt + 1)], t[:], x[:, tau, 256 * qt:256 * (qt + 1)],
                                         ALU.add),
                       reads=[bfq, x1l_buf], writes=[y_buf])
            sch.op("v", lambda eng, i=y_t, st=st_t: eng.bn_stats(st[:, 0:6], i[:, 0:512]),
                   reads=[y_buf], writes=[st_buf])
            sch.op("v", lambda eng, i=y_t, st=st_t: eng.bn_stats(st[:, 6:12], i[:, 512:1024]),
                   reads=[y_buf], writes=[st_buf])
            sch.op("v", lambda eng, st=st_t, tau=tau: eng.bn_aggr(fin[:, 2 * tau:2 * tau + 2], st[:, 0:12]),
                   reads=[st_buf], writes=[fin_b])
        sch.op("v", lambda eng: eng.tensor_scalar(
            fin[:, 8:12], fin[:].rearrange("p (t k) -> p k t", k=2)[:, 0, 0:4], -1.0, None, ALU.mult),
            reads=[fin_b], writes=[fin_b])
        sch.op("s", lambda eng: eng.activation(
            fin[:, 12:16], fin[:].rearrange("p (t k) -> p k t", k=2)[:, 1, 0:4], AF.Sqrt, bias=epsc[:]),
            reads=[fin_b, epsc_b], writes=[fin_b])
        sch.op("v", lambda eng: eng.reciprocal(fin[:, 16:20], fin[:, 12:16]),
               reads=[fin_b], writes=[fin_b])
        for tau in range(4):
            g = 4 * b + tau
            y_t, y_buf = yf[tau]
            o_t, o_buf = ot[tau % 2]
            sch.op("v", lambda eng, o=o_t, i=y_t, tau=tau:
                   eng.tensor_scalar(o[:], i[:], fin[:, 8 + tau:9 + tau], fin[:, 16 + tau:17 + tau],
                                     ALU.add, ALU.mult),
                   reads=[y_buf, fin_b], writes=[o_buf])
            load("q", out_d[128 * g:128 * (g + 1), :], o_t[:], [o_buf], [out_b])

    sch.drain_dmas()
    sch.emit_block()
    p4.close()
    p4w.close()
    root.close()
    return nc


def _host_constants():
    pos = np.arange(S, dtype=np.float64)
    cos = np.ones((128, S), dtype=np.float64)
    sin = np.zeros((128, S), dtype=np.float64)
    for p in range(128):
        dd = p % 64
        if dd < ROPE:
            j = dd // 2
            invf = 10000.0 ** (-(2.0 * j) / ROPE)
            ang = pos * invf
            cos[p] = np.cos(ang)
            sin[p] = np.sin(ang)
    r128 = np.zeros((128, 128), dtype=np.float32)
    for blk in range(2):
        o = 64 * blk
        for j in range(ROPE // 2):
            r128[o + 2 * j + 1, o + 2 * j] = -1.0
            r128[o + 2 * j, o + 2 * j + 1] = 1.0
    tri = (np.arange(128)[:, None] <= np.arange(128)[None, :]).astype(np.float32)
    return (cos.astype(np.float32), sin.astype(np.float32),
            r128.astype(BF), tri.astype(BF))


_NC_CACHE = [None]


def kernel(L0_attn_hidden_states, L0_attn_w_qkv, L0_attn_w_out,
           L0_ffn_cn_w1, L0_ffn_cn_b1, L0_ffn_cn_w2, L0_ffn_cn_b2,
           L0_ffn_cn_w3, L0_ffn_cn_b3, L0_ffn_cn_w4, L0_ffn_cn_b4,
           L0_ffn_gc_w1, L0_ffn_gc_b1, L0_ffn_gc_w2, L0_ffn_gc_b2,
           L0_ffn_gc_w3, L0_ffn_gc_b3, L0_ffn_adaptive_range):
    x = np.ascontiguousarray(np.asarray(L0_attn_hidden_states, dtype=np.float32))
    w_qkv = np.asarray(L0_attn_w_qkv, dtype=np.float32)
    w_out = np.asarray(L0_attn_w_out, dtype=np.float32)

    perm = np.empty(3 * D, dtype=np.int64)
    for c in range(3 * D):
        kind, r = divmod(c, D)
        h, dd = divmod(r, HD)
        base = (h // 4) * 768 + (h % 4) * 64 + dd
        perm[c] = base + (0 if kind == 0 else (512 if kind == 1 else 256))
    wqkvT = np.ascontiguousarray(w_qkv[perm, :].T)
    woutT = np.ascontiguousarray(w_out.T)

    rep = lambda a: np.ascontiguousarray(np.repeat(np.asarray(a, np.float32).reshape(1, -1), 128, axis=0))
    colmaj = lambda a, k: np.ascontiguousarray(np.asarray(a, np.float32).reshape(k, 128).T)

    cos_f, sin_f, r128, tri = _host_constants()
    in_common = dict(
        wqkvT=wqkvT, woutT=woutT,
        cnw1=np.ascontiguousarray(np.asarray(L0_ffn_cn_w1, np.float32)),
        cnw2=np.ascontiguousarray(np.asarray(L0_ffn_cn_w2, np.float32)),
        cnw3=np.ascontiguousarray(np.asarray(L0_ffn_cn_w3, np.float32)),
        cnw4=colmaj(L0_ffn_cn_w4, 2),
        gcw1=np.ascontiguousarray(np.asarray(L0_ffn_gc_w1, np.float32)),
        gcw2=np.ascontiguousarray(np.asarray(L0_ffn_gc_w2, np.float32)),
        gcw3r=np.ascontiguousarray(np.repeat(np.asarray(L0_ffn_gc_w3, np.float32).reshape(256, 1), 128, axis=1)),
        b1=colmaj(L0_ffn_cn_b1, 8), b2=colmaj(L0_ffn_cn_b2, 4), b3=colmaj(L0_ffn_cn_b3, 2),
        b4=rep(L0_ffn_cn_b4).reshape(128, 1),
        g1=colmaj(L0_ffn_gc_b1, 4), g2=colmaj(L0_ffn_gc_b2, 2),
        g3=rep(L0_ffn_gc_b3).reshape(128, 1),
        ar=rep(L0_ffn_adaptive_range).reshape(128, 1),
        identb=np.eye(128, dtype=BF), identf=np.eye(128, dtype=np.float32),
        tri=tri, onescol=np.ones((128, 1), dtype=BF),
        cosf=cos_f, sinf=sin_f, r128t=r128,
    )
    if _NC_CACHE[0] is None:
        _NC_CACHE[0] = build_nc()
    nc = _NC_CACHE[0]
    in_maps = [dict(in_common, x=np.ascontiguousarray(x[i])) for i in range(8)]
    res = run_bass_kernel_spmd(nc, in_maps, core_ids=list(range(8)))
    return np.stack([np.asarray(r["out"], dtype=np.float32) for r in res.results], axis=0)


if __name__ == "__main__":
    np.random.seed(0)
    import reference as _ref
    inputs = _ref.setup_inputs(0)
    got = kernel(**{k: np.asarray(v) for k, v in inputs.items()})
    want = np.asarray(_ref.reference(**inputs))
    num = float(np.mean((got - want) ** 2))
    den = float(np.mean(want ** 2))
    print("resid-var ratio:", num / den)


# version 18
# speedup vs baseline: 1.1670x; 1.0193x over previous
import numpy as np
import ml_dtypes
from contextlib import ExitStack

import concourse.bass as bass
import concourse.mybir as mybir
from concourse.bass_utils import run_bass_kernel_spmd

F32 = mybir.dt.float32
BF16 = mybir.dt.bfloat16
AF = mybir.ActivationFunctionType
ALU = mybir.AluOpType
AX = mybir.AxisListType

S = 4096
D = 1024
H = 16
HD = 64
ROPE = 32
NB = 8
NT = 32
EPS = 1e-5
BF = ml_dtypes.bfloat16


class Buf:
    __slots__ = ("w", "r")

    def __init__(self):
        self.w = None
        self.r = {}


class Sched:
    ENGNAME = {"t": "tensor", "v": "vector", "s": "scalar", "p": "gpsimd", "q": "sync"}

    NL = 8

    def __init__(self, nc, stack):
        self.nc = nc
        self.semkeys = (["t", "v", "s", "p"]
                        + ["q%d" % i for i in range(self.NL)]
                        + ["p%d" % i for i in range(self.NL)]
                        + ["s%d" % i for i in range(4)])
        self.sems = {k: stack.enter_context(nc.semaphore("sem_" + k)) for k in self.semkeys}
        self.cnt = {k: 0 for k in self.semkeys}
        self.waited = {e: {k: 0 for k in self.semkeys} for e in "tvspq"}
        self.thunks = {e: [] for e in "tvspq"}
        self.dmaidx = {"q": 0, "p": 0, "s": 0}

    def _wait(self, e, semkey, val):
        if val <= 0:
            return
        if e == "t" and semkey == "t":
            return
        if self.waited[e][semkey] >= val:
            return
        self.waited[e][semkey] = val
        sem = self.sems[semkey]
        self.thunks[e].append(lambda eng, s=sem, v=val: eng.wait_ge(s, v))

    def op(self, e, emit, reads=(), writes=(), dma=False):
        for b in reads:
            if b.w is not None:
                self._wait(e, b.w[0], b.w[1])
        for b in writes:
            if b.w is not None:
                self._wait(e, b.w[0], b.w[1])
            for sk, v in b.r.items():
                self._wait(e, sk, v)
        if dma:
            nl = 4 if e == "s" else self.NL
            lane = self.dmaidx[e] % nl
            self.dmaidx[e] += 1
            semkey = "%s%d" % (e, lane)
            inc = 16
            self._wait(e, semkey, self.cnt[semkey])
        else:
            semkey = e
            inc = 1
        self.cnt[semkey] += inc
        tag = (semkey, self.cnt[semkey])
        sem = self.sems[semkey]
        self.thunks[e].append(lambda eng, f=emit, s=sem, i=inc: f(eng).then_inc(s, i))
        for b in writes:
            b.w = tag
            b.r = {}
        for b in reads:
            if b.r.get(tag[0], 0) < tag[1]:
                b.r[tag[0]] = tag[1]
        return tag

    def drain_dmas(self):
        for i in range(self.NL):
            self._wait("q", "q%d" % i, self.cnt["q%d" % i])
            self._wait("q", "p%d" % i, self.cnt["p%d" % i])
        for i in range(4):
            self._wait("q", "s%d" % i, self.cnt["s%d" % i])

    def emit_block(self):
        with self.nc.Block() as blk:
            for e in "tvspq":
                if not self.thunks[e]:
                    continue

                def runner(eng, ths=self.thunks[e]):
                    for th in ths:
                        th(eng)

                getattr(blk, self.ENGNAME[e])(runner)
        self.thunks = {e: [] for e in "tvspq"}


def build_nc():
    nc = bass.Bass("TRN2", target_bir_lowering=False)

    x_d = nc.dram_tensor("x", [S, D], F32, kind="ExternalInput").ap()
    wq_d = nc.dram_tensor("wqkvT", [D, 3 * D], F32, kind="ExternalInput").ap()
    wo_d = nc.dram_tensor("woutT", [D, D], F32, kind="ExternalInput").ap()
    cw1_d = nc.dram_tensor("cnw1", [D, D], F32, kind="ExternalInput").ap()
    cw2_d = nc.dram_tensor("cnw2", [D, 512], F32, kind="ExternalInput").ap()
    cw3_d = nc.dram_tensor("cnw3", [512, 256], F32, kind="ExternalInput").ap()
    cw4_d = nc.dram_tensor("cnw4", [128, 2], F32, kind="ExternalInput").ap()
    gw1_d = nc.dram_tensor("gcw1", [D, 512], F32, kind="ExternalInput").ap()
    gw2_d = nc.dram_tensor("gcw2", [512, 256], F32, kind="ExternalInput").ap()
    gw3_d = nc.dram_tensor("gcw3r", [256, 128], F32, kind="ExternalInput").ap()
    b1_d = nc.dram_tensor("b1", [128, 8], F32, kind="ExternalInput").ap()
    b2_d = nc.dram_tensor("b2", [128, 4], F32, kind="ExternalInput").ap()
    b3_d = nc.dram_tensor("b3", [128, 2], F32, kind="ExternalInput").ap()
    b4_d = nc.dram_tensor("b4", [128, 1], F32, kind="ExternalInput").ap()
    g1_d = nc.dram_tensor("g1", [128, 4], F32, kind="ExternalInput").ap()
    g2_d = nc.dram_tensor("g2", [128, 2], F32, kind="ExternalInput").ap()
    g3_d = nc.dram_tensor("g3", [128, 1], F32, kind="ExternalInput").ap()
    ar_d = nc.dram_tensor("ar", [128, 1], F32, kind="ExternalInput").ap()
    idb_d = nc.dram_tensor("identb", [128, 128], BF16, kind="ExternalInput").ap()
    idf_d = nc.dram_tensor("identf", [128, 128], F32, kind="ExternalInput").ap()
    tri_d = nc.dram_tensor("tri", [128, 128], BF16, kind="ExternalInput").ap()
    ones_d = nc.dram_tensor("onescol", [128, 1], BF16, kind="ExternalInput").ap()
    cos_d = nc.dram_tensor("cosf", [128, S], F32, kind="ExternalInput").ap()
    sin_d = nc.dram_tensor("sinf", [128, S], F32, kind="ExternalInput").ap()
    r128_d = nc.dram_tensor("r128t", [128, 128], BF16, kind="ExternalInput").ap()

    qf_d = nc.dram_tensor("qf", [D, S], BF16, kind="Internal").ap()
    kf_d = nc.dram_tensor("kf", [D, S], BF16, kind="Internal").ap()
    vt_d = nc.dram_tensor("vt", [S, D], BF16, kind="Internal").ap()
    x1_d = nc.dram_tensor("x1", [S, D], F32, kind="Internal").ap()
    out_d = nc.dram_tensor("out", [S, D], F32, kind="ExternalOutput").ap()

    root = ExitStack()
    sch = Sched(nc, root)

    def sb(name, shape, dtype, stk):
        t = stk.enter_context(nc.sbuf_tensor("sb_" + name, shape, dtype))
        return t, Buf()

    def ps(name, shape, dtype, stk):
        t = stk.enter_context(nc.psum_tensor("ps_" + name, shape, dtype))
        return t, Buf()

    qf_b, kf_b, vt_b, x1_b, out_b = Buf(), Buf(), Buf(), Buf(), Buf()
    xd_b = Buf()

    identb, identb_b = sb("identb", [128, 128], BF16, root)
    identf, identf_b = sb("identf", [128, 128], F32, root)
    tri, tri_b = sb("tri", [128, 128], BF16, root)
    onescol, onescol_b = sb("onescol", [128, 1], BF16, root)
    b1s, b1s_b = sb("b1s", [128, 8], F32, root)
    b2s, b2s_b = sb("b2s", [128, 4], F32, root)
    b3s, b3s_b = sb("b3s", [128, 2], F32, root)
    b4s, b4s_b = sb("b4s", [128, 1], F32, root)
    g1s, g1s_b = sb("g1s", [128, 4], F32, root)
    g2s, g2s_b = sb("g2s", [128, 2], F32, root)
    g3s, g3s_b = sb("g3s", [128, 1], F32, root)
    ars, ars_b = sb("ars", [128, 1], F32, root)
    epsc, epsc_b = sb("epsc", [128, 1], F32, root)
    gcf, gcf_b = sb("gcf", [128, 8], F32, root)

    def load(e, dst_ap, src_ap, rbufs, wbufs):
        sch.op(e, lambda eng, d=dst_ap, s=src_ap: eng.dma_start(d, s),
               reads=rbufs, writes=wbufs, dma=True)

    for dst, dstb, src in [
        (identb, identb_b, idb_d), (identf, identf_b, idf_d), (tri, tri_b, tri_d),
        (onescol, onescol_b, ones_d), (b1s, b1s_b, b1_d), (b2s, b2s_b, b2_d),
        (b3s, b3s_b, b3_d), (b4s, b4s_b, b4_d), (g1s, g1s_b, g1_d),
        (g2s, g2s_b, g2_d), (g3s, g3s_b, g3_d), (ars, ars_b, ar_d),
    ]:
        load("q", dst[:], src[:], [], [dstb])
    sch.op("v", lambda eng: eng.memset(epsc[:], EPS), reads=[], writes=[epsc_b])

    p1 = ExitStack()
    wq, wq_b = sb("wq", [128, 8, 3 * D], BF16, p1)
    cosf, cosf_b = sb("cosf", [128, S], F32, p1)
    sinf, sinf_b = sb("sinf", [128, S], F32, p1)
    r128, r128_b = sb("r128", [128, 128], BF16, p1)
    load("q", r128[:], r128_d[:], [], [r128_b])
    load("q", cosf[:], cos_d[:], [], [cosf_b])
    load("q", sinf[:], sin_d[:], [], [sinf_b])
    wq_src = wq_d.rearrange("(c p) f -> p c f", p=128)
    load("p", wq[:, 0:4, :], wq_src[:, 0:4, :], [], [wq_b])
    load("p", wq[:, 4:8, :], wq_src[:, 4:8, :], [], [wq_b])

    xt = [sb("xt%d" % i, [128, D], F32, p1) for i in range(2)]
    xtb = [sb("xtb%d" % i, [128, D], BF16, p1) for i in range(2)]
    xf = [sb("xf%d" % i, [128, 8, 512], BF16, p1) for i in range(2)]
    qkraw, qkraw_b = sb("qkraw", [128, 16, 512], BF16, p1)
    qkraw_bufs = [Buf() for _ in range(16)]
    qkro = [sb("qkro%d" % i, [128, 16, 512], BF16, p1) for i in range(2)]
    vraw = [sb("vraw%d" % i, [128, 512], BF16, p1) for i in range(2)]
    vsb, vsb_b = sb("vsb", [128, 8, 4, 128], BF16, p1)
    tsin = [sb("tsin%d" % i, [128, 512], F32, p1) for i in range(2)]
    tcos = [sb("tcos%d" % i, [128, 512], F32, p1) for i in range(2)]

    trp = [ps("trp%d" % i, [128, 1024], BF16, p1) for i in range(2)]
    qps = [ps("qps%d" % i, [128, 512], F32, p1) for i in range(2)]
    rp = [ps("rp%d" % i, [128, 512], F32, p1) for i in range(2)]
    vtp, vtp_b = ps("vtp", [128, 512], BF16, p1)

    for b in range(NB):
        xfb, xfb_buf = xf[b % 2]
        qkro_t, qkro_buf = qkro[b % 2]
        for tau in range(4):
            g = 4 * b + tau
            xt_t, xt_buf = xt[g % 2]
            xtb_t, xtb_buf = xtb[g % 2]
            trp_t, trp_buf = trp[g % 2]
            load("q", xt_t[:], x_d[128 * g:128 * (g + 1), :], [xd_b], [xt_buf])
            sch.op("s", lambda eng, o=xtb_t, i=xt_t: eng.copy(o[:], i[:]),
                   reads=[xt_buf], writes=[xtb_buf])
            for c in range(8):
                sch.op("t", lambda eng, o=trp_t, i=xtb_t, c=c:
                       eng.transpose(o[:, 128 * c:128 * (c + 1)], i[:, 128 * c:128 * (c + 1)], identb[:]),
                       reads=[xtb_buf, identb_b], writes=[trp_buf])
            dst = xfb[:].rearrange("p c (t4 e) -> p c t4 e", t4=4)[:, :, tau, :]
            src = trp_t[:].rearrange("p (c e) -> p c e", c=8)
            sch.op("v", lambda eng, d=dst, s=src: eng.tensor_copy(d, s),
                   reads=[trp_buf], writes=[xfb_buf])
        for fo in range(24):
            qp_t, qp_buf = qps[fo % 2]
            for p in range(8):
                sch.op("t", lambda eng, o=qp_t, fo=fo, p=p, x=xfb:
                       eng.matmul(o[:], wq[:, p, 128 * fo:128 * (fo + 1)], x[:, p, :],
                                  start=(p == 0), stop=(p == 7), skip_group_check=True),
                       reads=[wq_b, xfb_buf], writes=[qp_buf])
            if fo < 16:
                c = fo
                sch.op("v", lambda eng, c=c, i=qp_t: eng.tensor_copy(qkraw[:, c, :], i[:]),
                       reads=[qp_buf], writes=[qkraw_bufs[c]])
            else:
                c = fo - 16
                vr_t, vr_buf = vraw[c % 2]
                sch.op("v", lambda eng, o=vr_t, i=qp_t: eng.tensor_copy(o[:], i[:]),
                       reads=[qp_buf], writes=[vr_buf])
                for tau in range(4):
                    sch.op("t", lambda eng, i=vr_t, tau=tau:
                           eng.transpose(vtp[:, 128 * tau:128 * (tau + 1)], i[:, 128 * tau:128 * (tau + 1)], identb[:]),
                           reads=[vr_buf, identb_b], writes=[vtp_b])
                sch.op("v", lambda eng, c=c: eng.tensor_copy(
                    vsb[:, c, :, :], vtp[:].rearrange("p (t4 e) -> p t4 e", t4=4)),
                    reads=[vtp_b], writes=[vsb_b])
        for c in range(16):
            rp_t, rp_buf = rp[c % 2]
            ts_t, ts_buf = tsin[c % 2]
            tc_t, tc_buf = tcos[c % 2]
            sch.op("t", lambda eng, o=rp_t, c=c:
                   eng.matmul(o[:], r128[:], qkraw[:, c, :], start=True, stop=True, skip_group_check=True),
                   reads=[r128_b, qkraw_bufs[c]], writes=[rp_buf])
            sch.op("v", lambda eng, o=ts_t, i=rp_t, b=b:
                   eng.tensor_tensor(o[:], i[:], sinf[:, 512 * b:512 * (b + 1)], ALU.mult),
                   reads=[rp_buf, sinf_b], writes=[ts_buf])
            sch.op("p", lambda eng, o=tc_t, c=c, b=b:
                   eng.tensor_tensor(o[:], qkraw[:, c, :], cosf[:, 512 * b:512 * (b + 1)], ALU.mult),
                   reads=[qkraw_bufs[c], cosf_b], writes=[tc_buf])
            sch.op("p", lambda eng, o=qkro_t, c=c, i0=tc_t, i1=ts_t:
                   eng.tensor_tensor(o[:, c, :], i0[:], i1[:], ALU.add),
                   reads=[tc_buf, ts_buf], writes=[qkro_buf])
        qdst = qf_d.rearrange("(c p) s -> p c s", p=128)[:, :, 512 * b:512 * (b + 1)]
        kdst = kf_d.rearrange("(c p) s -> p c s", p=128)[:, :, 512 * b:512 * (b + 1)]
        vdst = vt_d.rearrange("(B t4 p) (c e) -> B p c t4 e", t4=4, p=128, e=128)[b]
        sch.op("p", lambda eng, d=qdst, s=qkro_t: eng.dma_start(d, s[:, 0:8, :]),
               reads=[qkro_buf], writes=[qf_b], dma=True)
        sch.op("p", lambda eng, d=kdst, s=qkro_t: eng.dma_start(d, s[:, 8:16, :]),
               reads=[qkro_buf], writes=[kf_b], dma=True)
        sch.op("p", lambda eng, d=vdst: eng.dma_start(d, vsb[:]),
               reads=[vsb_b], writes=[vt_b], dma=True)

    sch.drain_dmas()
    sch.emit_block()
    p1.close()

    p23 = ExitStack()
    aot, aot_b = sb("aot", [128, NT, D], BF16, p23)
    wo, wo_b = sb("wo", [128, 8, D], BF16, p23)
    load("p", wo[:], wo_d.rearrange("(c p) f -> p c f", p=128), [], [wo_b])

    p2 = ExitStack()
    kst = [sb("kst%d" % i, [128, S], BF16, p2) for i in range(4)]
    qst = [sb("qst%d" % i, [128, S], BF16, p2) for i in range(4)]
    vst = [sb("vst%d" % i, [128, NT, 130], BF16, p2) for i in range(2)]
    Ppair = [[sb("Pp%d_%d" % (h2, i), [128, 1024], BF16, p2) for i in range(3)] for h2 in range(2)]
    rcp = [sb("rcp%d" % i, [128, 4], F32, p2) for i in range(4)]
    pairps = [ps("pair%d" % i, [128, 1024], F32, p2) for i in range(2)]
    pvps = [[ps("pv%d_%d" % (h2, i), [128, 260], F32, p2) for i in range(2)] for h2 in range(2)]

    for i in range(4):
        t, bfk = kst[i]
        sch.op("v", lambda eng, t=t: eng.memset(t[64:128, :], 0.0), reads=[], writes=[bfk])
        t, bfq2 = qst[i]
        sch.op("v", lambda eng, t=t: eng.memset(t[64:128, :], 0.0), reads=[], writes=[bfq2])
    for i in range(2):
        t, bfv = vst[i]
        sch.op("v", lambda eng, t=t: eng.memset(t[:].rearrange("p i (h2 e) -> p i h2 e", e=65)[:, :, :, 64], 1.0),
               reads=[], writes=[bfv])

    vt_src = vt_d.rearrange("(i p) e -> p i e", p=128)
    for u in range(H // 2):
        ub = u % 2
        ks = [kst[2 * ub], kst[2 * ub + 1]]
        qs = [qst[2 * ub], qst[2 * ub + 1]]
        vst_t, vst_buf = vst[ub]
        for h2 in (0, 1):
            h = 2 * u + h2
            load("q", ks[h2][0][0:64, :], kf_d[64 * h:64 * (h + 1), :], [kf_b], [ks[h2][1]])
            load("q", qs[h2][0][0:64, :], qf_d[64 * h:64 * (h + 1), :], [qf_b], [qs[h2][1]])
        vv = vst_t[:].rearrange("p i (h2 e) -> p i h2 e", e=65)
        load("q", vv[:, :, 0, 0:64], vt_src[:, :, 128 * u:128 * u + 64], [vt_b], [vst_buf])
        load("q", vv[:, :, 1, 0:64], vt_src[:, :, 128 * u + 64:128 * u + 128], [vt_b], [vst_buf])
        for sbk in range(NB):
            nI = 4 * sbk + 4
            npair = nI // 2
            pvs = [pvps[0][sbk % 2], pvps[1][sbk % 2]]
            first = [True, True]

            def emit_pv_pair(pp, h2, sbk=sbk, nI=nI, pvs=pvs, first=first,
                             vst_t=vst_t, vst_buf=vst_buf):
                Pp_t, Pp_buf = Ppair[h2][pp % 3]
                pv_t, pv_buf = pvs[h2]
                for half in (0, 1):
                    i = 2 * pp + half
                    for jj in range(4):
                        if i > 4 * sbk + jj:
                            continue
                        st0 = first[h2]
                        first[h2] = False
                        last = (i == nI - 1) and (jj == 3)
                        sch.op("t", lambda eng, o=pv_t, P=Pp_t, half=half, jj=jj, i=i, h2=h2,
                               st0=st0, last=last, v=vst_t:
                               eng.matmul(o[:, 65 * jj:65 * jj + 65],
                                          P[:, 512 * half + 128 * jj:512 * half + 128 * (jj + 1)],
                                          v[:, i, 65 * h2:65 * h2 + 65],
                                          start=st0, stop=last, skip_group_check=True),
                               reads=[Pp_buf, vst_buf], writes=[pv_buf])

            for pp in range(npair):
                for h2 in (0, 1):
                    pa_t, pa_buf = pairps[h2]
                    Pp_t, Pp_buf = Ppair[h2][pp % 3]
                    kst_t, kst_buf = ks[h2]
                    qst_t, qst_buf = qs[h2]
                    for half in (0, 1):
                        i = 2 * pp + half
                        sch.op("t", lambda eng, o=pa_t, half=half, i=i, k=kst_t, q=qst_t, sbk=sbk:
                               eng.matmul(o[:, 512 * half:512 * (half + 1)], k[:, 128 * i:128 * (i + 1)],
                                          q[:, 512 * sbk:512 * (sbk + 1)],
                                          start=True, stop=True, skip_group_check=True),
                               reads=[kst_buf, qst_buf], writes=[pa_buf])
                    r0 = 2 * pp - 4 * sbk
                    if r0 >= 0:
                        for half in (0, 1):
                            r = r0 + half
                            c0 = 512 * half + 128 * r
                            sch.op("s", lambda eng, o=Pp_t, i2=pa_t, c0=c0, c1=512 * (half + 1):
                                   eng.activation(o[:, c0:c1], i2[:, c0:c1], AF.Exp, scale=0.125),
                                   reads=[pa_buf], writes=[Pp_buf])
                            sch.op("v", lambda eng, P=Pp_t, c0=c0:
                                   eng.tensor_tensor(P[:, c0:c0 + 128], P[:, c0:c0 + 128], tri[:], ALU.mult),
                                   reads=[Pp_buf, tri_b], writes=[Pp_buf])
                    else:
                        sch.op("s", lambda eng, o=Pp_t, i2=pa_t:
                               eng.activation(o[:], i2[:], AF.Exp, scale=0.125),
                               reads=[pa_buf], writes=[Pp_buf])
                if pp >= 1:
                    for h2 in (0, 1):
                        emit_pv_pair(pp - 1, h2)
            for h2 in (0, 1):
                emit_pv_pair(npair - 1, h2)
            for h2 in (0, 1):
                pv_t, pv_buf = pvs[h2]
                rc_t, rc_buf = rcp[2 * (sbk % 2) + h2]
                sch.op("v", lambda eng, o=rc_t, i=pv_t:
                       eng.reciprocal(o[:], i[:].rearrange("p (j e) -> p j e", e=65)[:, :, 64]),
                       reads=[pv_buf], writes=[rc_buf])
                h = 2 * u + h2
                for jj in range(4):
                    tau = 4 * sbk + jj
                    sch.op("v", lambda eng, i=pv_t, jj=jj, tau=tau, h=h, rc=rc_t:
                           eng.tensor_scalar(aot[:, tau, 64 * h:64 * (h + 1)],
                                             i[:, 65 * jj:65 * jj + 64], rc[:, jj:jj + 1], None, ALU.mult),
                           reads=[pv_buf, rc_buf], writes=[aot_b])

    sch.drain_dmas()
    sch.emit_block()
    p2.close()

    p3 = ExitStack()
    af = [sb("af%d" % i, [128, 1024], BF16, p3) for i in range(2)]
    xin = [sb("xin%d" % i, [128, D], F32, p3) for i in range(2)]
    yv = [sb("y%d" % i, [128, D], F32, p3) for i in range(2)]
    x1t_ = [sb("x1t%d" % i, [128, D], F32, p3) for i in range(2)]
    x1bf = [sb("x1bf%d" % i, [128, D], BF16, p3) for i in range(2)]
    stats = [sb("stats%d" % i, [128, 18], F32, p3) for i in range(2)]
    atp = [ps("atp%d" % i, [128, 1024], BF16, p3) for i in range(2)]
    wop = [ps("wop%d" % i, [128, 1024], F32, p3) for i in range(2)]
    gcp, gcp_b = ps("gcp", [128, 8], F32, p3)

    def emit_gcsum(g, x1b_t, x1b_buf):
        for c in range(8):
            sch.op("t", lambda eng, g=g, c=c, xb=x1b_t:
                   eng.matmul(gcp[:, c:c + 1], xb[:, 128 * c:128 * (c + 1)], onescol[:],
                              start=(g == 0 and c == 0), stop=(g == NT - 1 and c == 7), skip_group_check=True),
                   reads=[x1b_buf, onescol_b], writes=[gcp_b])

    for g in range(NT):
        af_t, af_buf = af[g % 2]
        xin_t, xin_buf = xin[g % 2]
        y_t, y_buf = yv[g % 2]
        x1_t, x1_buf = x1t_[g % 2]
        x1b_t, x1b_buf = x1bf[g % 2]
        st_t, st_buf = stats[g % 2]
        atp_t, atp_buf = atp[g % 2]
        wop_t, wop_buf = wop[g % 2]
        load("q", xin_t[:], x_d[128 * g:128 * (g + 1), :], [xd_b], [xin_buf])
        for c in range(8):
            sch.op("t", lambda eng, o=atp_t, g=g, c=c:
                   eng.transpose(o[:, 128 * c:128 * (c + 1)], aot[:, g, 128 * c:128 * (c + 1)], identb[:]),
                   reads=[aot_b, identb_b], writes=[atp_buf])
        sch.op("v", lambda eng, o=af_t, i=atp_t: eng.tensor_copy(o[:], i[:]),
               reads=[atp_buf], writes=[af_buf])
        for n2 in range(2):
            for p in range(8):
                sch.op("t", lambda eng, o=wop_t, n2=n2, p=p, a=af_t:
                       eng.matmul(o[:, 512 * n2:512 * (n2 + 1)], a[:, 128 * p:128 * (p + 1)],
                                  wo[:, p, 512 * n2:512 * (n2 + 1)],
                                  start=(p == 0), stop=(p == 7), skip_group_check=True),
                       reads=[af_buf, wo_b], writes=[wop_buf])
        if g >= 1:
            emit_gcsum(g - 1, x1bf[(g - 1) % 2][0], x1bf[(g - 1) % 2][1])
        sch.op("v", lambda eng, o=y_t, i0=xin_t, i1=wop_t:
               eng.tensor_tensor(o[:], i0[:], i1[:], ALU.add),
               reads=[xin_buf, wop_buf], writes=[y_buf])
        sch.op("v", lambda eng, i=y_t, st=st_t: eng.bn_stats(st[:, 0:6], i[:, 0:512]),
               reads=[y_buf], writes=[st_buf])
        sch.op("v", lambda eng, i=y_t, st=st_t: eng.bn_stats(st[:, 6:12], i[:, 512:1024]),
               reads=[y_buf], writes=[st_buf])
        sch.op("v", lambda eng, st=st_t: eng.bn_aggr(st[:, 12:14], st[:, 0:12]),
               reads=[st_buf], writes=[st_buf])
        sch.op("v", lambda eng, st=st_t: eng.tensor_scalar(st[:, 14:15], st[:, 12:13], -1.0, None, ALU.mult),
               reads=[st_buf], writes=[st_buf])
        sch.op("s", lambda eng, st=st_t: eng.activation(st[:, 15:16], st[:, 13:14], AF.Sqrt, bias=epsc[:]),
               reads=[st_buf, epsc_b], writes=[st_buf])
        sch.op("v", lambda eng, st=st_t: eng.reciprocal(st[:, 16:17], st[:, 15:16]),
               reads=[st_buf], writes=[st_buf])
        sch.op("v", lambda eng, o=x1_t, i=y_t, st=st_t:
               eng.tensor_scalar(o[:], i[:], st[:, 14:15], st[:, 16:17], ALU.add, ALU.mult),
               reads=[y_buf, st_buf], writes=[x1_buf])
        sch.op("s", lambda eng, o=x1b_t, i=x1_t: eng.copy(o[:], i[:]),
               reads=[x1_buf], writes=[x1b_buf])
        load("q", x1_d[128 * g:128 * (g + 1), :], x1_t[:], [x1_buf], [x1_b])


    emit_gcsum(NT - 1, x1bf[(NT - 1) % 2][0], x1bf[(NT - 1) % 2][1])
    sch.op("v", lambda eng: eng.tensor_scalar(gcf[:], gcp[:], 1.0 / S, None, ALU.mult),
           reads=[gcp_b], writes=[gcf_b])
    sch.drain_dmas()
    sch.emit_block()
    p3.close()
    p23.close()

    p4w = ExitStack()
    cw1, cw1_b = sb("cw1", [128, 8, D], BF16, p4w)
    cw2, cw2_b = sb("cw2", [128, 8, 512], BF16, p4w)
    cw3, cw3_b = sb("cw3", [128, 4, 256], BF16, p4w)
    cw4, cw4_b = sb("cw4", [128, 2], BF16, p4w)
    gw1, gw1_b = sb("gw1", [128, 8, 512], BF16, p4w)
    gw2, gw2_b = sb("gw2", [128, 4, 256], BF16, p4w)
    gw3, gw3_b = sb("gw3", [128, 2, 128], BF16, p4w)
    load("p", gw1[:], gw1_d.rearrange("(c p) f -> p c f", p=128), [], [gw1_b])
    load("p", gw2[:], gw2_d.rearrange("(c p) f -> p c f", p=128), [], [gw2_b])
    load("p", gw3[:], gw3_d.rearrange("(c p) f -> p c f", p=128), [], [gw3_b])
    load("p", cw1[:], cw1_d.rearrange("(c p) f -> p c f", p=128), [], [cw1_b])
    load("p", cw2[:], cw2_d.rearrange("(c p) f -> p c f", p=128), [], [cw2_b])
    load("p", cw3[:], cw3_d.rearrange("(c p) f -> p c f", p=128), [], [cw3_b])
    load("p", cw4[:], cw4_d[:], [], [cw4_b])

    p4 = ExitStack()
    gcfb, gcfb_b = sb("gcfb", [128, 8], BF16, p4)
    h1g, h1g_b = sb("h1g", [128, 4], BF16, p4)
    h2g, h2g_b = sb("h2g", [128, 2], BF16, p4)
    gfv, gfv_b = sb("gfv", [128, 4], F32, p4)
    x1l = [sb("x1l%d" % i, [128, 4, D], F32, p4) for i in range(2)]
    quarters = [sb("qq%d" % i, [128, 256], F32, p4) for i in range(36)]
    ofs = [sb("of%d" % i, [128, 8, 512], BF16, p4) for i in range(2)]
    h1s = [sb("h1_%d" % i, [128, 8, 512], BF16, p4) for i in range(2)]
    h2s = [sb("h2_%d" % i, [128, 4, 512], BF16, p4) for i in range(2)]
    h3s = [sb("h3_%d" % i, [128, 2, 512], BF16, p4) for i in range(2)]
    tfcs = [sb("tfc%d" % i, [128, 8], F32, p4) for i in range(2)]
    yf = [sb("yf%d" % i, [128, D], F32, p4) for i in range(4)]
    ot = [sb("ot%d" % i, [128, D], F32, p4) for i in range(2)]
    st4 = [sb("st4_%d" % i, [128, 12], F32, p4) for i in range(2)]
    fin, fin_b = sb("fin", [128, 20], F32, p4)
    ofp = [ps("ofp%d" % i, [128, 512], F32, p4) for i in range(4)]
    cnp = [ps("cnp%d" % i, [128, 512], F32, p4) for i in range(2)]
    z4ps = [ps("z4p%d" % i, [128, 8], F32, p4) for i in range(2)]

    z4p, z4p_b = z4ps[0]
    sch.op("s", lambda eng: eng.copy(gcfb[:], gcf[:]), reads=[gcf_b], writes=[gcfb_b])
    for fo in range(4):
        for p in range(8):
            sch.op("t", lambda eng, fo=fo, p=p:
                   eng.matmul(z4p[:, fo:fo + 1], gw1[:, p, 128 * fo:128 * (fo + 1)], gcfb[:, p:p + 1],
                              start=(fo == 0 and p == 0), stop=(fo == 3 and p == 7), skip_group_check=True),
                   reads=[gw1_b, gcfb_b], writes=[z4p_b])
    for fo in range(4):
        sch.op("s", lambda eng, fo=fo:
               eng.activation(h1g[:, fo:fo + 1], z4p[:, fo:fo + 1], AF.Gelu, bias=g1s[:, fo:fo + 1]),
               reads=[z4p_b, g1s_b], writes=[h1g_b])
    for fo in range(2):
        for p in range(4):
            sch.op("t", lambda eng, fo=fo, p=p:
                   eng.matmul(z4p[:, 4 + fo:5 + fo], gw2[:, p, 128 * fo:128 * (fo + 1)], h1g[:, p:p + 1],
                              start=(fo == 0 and p == 0), stop=(fo == 1 and p == 3), skip_group_check=True),
                   reads=[gw2_b, h1g_b], writes=[z4p_b])
    for fo in range(2):
        sch.op("s", lambda eng, fo=fo:
               eng.activation(h2g[:, fo:fo + 1], z4p[:, 4 + fo:5 + fo], AF.Gelu, bias=g2s[:, fo:fo + 1]),
               reads=[z4p_b, g2s_b], writes=[h2g_b])
    for p in range(2):
        sch.op("t", lambda eng, p=p:
               eng.matmul(z4p[:, 6:7], gw3[:, p, :], h2g[:, p:p + 1],
                          start=(p == 0), stop=(p == 1), skip_group_check=True),
               reads=[gw3_b, h2g_b], writes=[z4p_b])
    sch.op("v", lambda eng: eng.tensor_scalar(gfv[:, 1:2], g3s[:], 0.5, None, ALU.mult),
           reads=[g3s_b], writes=[gfv_b])
    sch.op("s", lambda eng: eng.activation(gfv[:, 0:1], z4p[:, 6:7], AF.Tanh, bias=gfv[:, 1:2], scale=0.5),
           reads=[z4p_b, gfv_b], writes=[gfv_b])
    sch.op("v", lambda eng: eng.tensor_scalar(gfv[:, 1:2], gfv[:, 0:1], 0.7, -0.3, ALU.mult, ALU.add),
           reads=[gfv_b], writes=[gfv_b])
    sch.op("v", lambda eng: eng.tensor_scalar(gfv[:, 2:3], gfv[:, 1:2], ars[:], 0.1, ALU.mult, ALU.add),
           reads=[gfv_b, ars_b], writes=[gfv_b])
    sch.op("v", lambda eng: eng.tensor_scalar(gfv[:, 3:4], ars[:], 0.3, None, ALU.mult),
           reads=[ars_b], writes=[gfv_b])
    sch.op("v", lambda eng: eng.tensor_tensor(gfv[:, 2:3], gfv[:, 2:3], gfv[:, 3:4], ALU.add),
           reads=[gfv_b], writes=[gfv_b])
    b4h, b4h_b = sb("b4h", [128, 1], F32, p4)
    sch.op("v", lambda eng: eng.tensor_scalar(b4h[:], b4s[:], 0.5, None, ALU.mult),
           reads=[b4s_b], writes=[b4h_b])
    gfn, gfn_b = sb("gfn", [128, 2], F32, p4)
    sch.op("v", lambda eng: eng.tensor_scalar(gfn[:, 0:1], gfv[:, 2:3], -1.0, None, ALU.mult),
           reads=[gfv_b], writes=[gfn_b])
    sch.op("v", lambda eng: eng.tensor_scalar(gfn[:, 1:2], gfv[:, 3:4], -1.0, None, ALU.mult),
           reads=[gfv_b], writes=[gfn_b])

    x1_src = x1_d.rearrange("(b t4 p) e -> b p t4 e", p=128, t4=4)
    qpool = [quarters[18 * par:18 * par + 18] for par in range(2)]
    qbs = {}

    def stage_block(b):
        par = b % 2
        x1l_t, x1l_buf = x1l[par]
        load("q", x1l_t[:], x1_src[b], [x1_b], [x1l_buf])
        qb = [[None] * 4 for _ in range(4)]
        pool = qpool[par]
        idx = 0
        for tau in range(4):
            for qt in range(4):
                qb[tau][qt] = pool[idx]
                idx += 1
        for tau in range(4):
            for qt in range(4):
                t, bfq = qb[tau][qt]
                sch.op("s", lambda eng, t=t, tau=tau, qt=qt, x=x1l_t:
                       eng.copy(t[:], x[:, tau, 256 * qt:256 * (qt + 1)]),
                       reads=[x1l_buf], writes=[bfq])
        qbs[b] = (qb, [pool[16], pool[17]])

    stage_block(0)
    for b in range(NB):
        par = b % 2
        x1l_t, x1l_buf = x1l[par]
        qb, spares = qbs.pop(b)
        of, of_b = ofs[par]
        h1, h1_b = h1s[par]
        h2, h2_b = h2s[par]
        h3, h3_b = h3s[par]
        tfc, tfc_b = tfcs[par]
        z4p, z4p_b = z4ps[par]
        for cyc in range(3):
            for c in range(8):
                ofp_t, ofp_buf = ofp[c % 4]
                qt, off = c // 2, 128 * (c % 2)
                for tau in range(4):
                    t, bfq = qb[tau][qt]
                    sch.op("t", lambda eng, o=ofp_t, t=t, tau=tau, off=off:
                           eng.transpose(o[:, 128 * tau:128 * (tau + 1)], t[:, off:off + 128], identf[:]),
                           reads=[bfq, identf_b], writes=[ofp_buf])
                sch.op("s", lambda eng, c=c, s2=ofp_t, of=of: eng.copy(of[:, c, :], s2[:]),
                       reads=[ofp_buf], writes=[of_b])
            for fo in range(8):
                cn_t, cn_buf = cnp[fo % 2]
                for p in range(8):
                    sch.op("t", lambda eng, o=cn_t, fo=fo, p=p, of=of:
                           eng.matmul(o[:], cw1[:, p, 128 * fo:128 * (fo + 1)], of[:, p, :],
                                      start=(p == 0), stop=(p == 7), skip_group_check=True),
                           reads=[cw1_b, of_b], writes=[cn_buf])
                sch.op("s", lambda eng, o=h1, fo=fo, i=cn_t:
                       eng.activation(o[:, fo, :], i[:], AF.Gelu, bias=b1s[:, fo:fo + 1]),
                       reads=[cn_buf, b1s_b], writes=[h1_b])
            for fo in range(4):
                cn_t, cn_buf = cnp[fo % 2]
                for p in range(8):
                    sch.op("t", lambda eng, o=cn_t, fo=fo, p=p, h1=h1:
                           eng.matmul(o[:], cw2[:, p, 128 * fo:128 * (fo + 1)], h1[:, p, :],
                                      start=(p == 0), stop=(p == 7), skip_group_check=True),
                           reads=[cw2_b, h1_b], writes=[cn_buf])
                sch.op("s", lambda eng, o=h2, fo=fo, i=cn_t:
                       eng.activation(o[:, fo, :], i[:], AF.Gelu, bias=b2s[:, fo:fo + 1]),
                       reads=[cn_buf, b2s_b], writes=[h2_b])
            for fo in range(2):
                cn_t, cn_buf = cnp[fo % 2]
                for p in range(4):
                    sch.op("t", lambda eng, o=cn_t, fo=fo, p=p, h2=h2:
                           eng.matmul(o[:], cw3[:, p, 128 * fo:128 * (fo + 1)], h2[:, p, :],
                                      start=(p == 0), stop=(p == 3), skip_group_check=True),
                           reads=[cw3_b, h2_b], writes=[cn_buf])
                sch.op("s", lambda eng, o=h3, fo=fo, i=cn_t:
                       eng.activation(o[:, fo, :], i[:], AF.Gelu, bias=b3s[:, fo:fo + 1]),
                       reads=[cn_buf, b3s_b], writes=[h3_b])
            for tau in range(4):
                for p in range(2):
                    sch.op("t", lambda eng, tau=tau, p=p, h3=h3, z=z4p:
                           eng.matmul(z[:, tau:tau + 1], h3[:, p, 128 * tau:128 * (tau + 1)], cw4[:, p:p + 1],
                                      start=(tau == 0 and p == 0), stop=(tau == 3 and p == 1), skip_group_check=True),
                           reads=[h3_b, cw4_b], writes=[z4p_b])
            sch.op("s", lambda eng, tfc=tfc, z=z4p: eng.activation(tfc[:, 0:4], z[:, 0:4], AF.Tanh,
                                                                   bias=b4h[:], scale=0.5),
                   reads=[z4p_b, b4h_b], writes=[tfc_b])
            sch.op("v", lambda eng, tfc=tfc: eng.tensor_scalar(tfc[:, 4:8], tfc[:, 0:4], gfn[:, 1:2], gfn[:, 0:1],
                                                               ALU.mult, ALU.add),
                   reads=[tfc_b, gfn_b], writes=[tfc_b])
            sch.op("v", lambda eng, tfc=tfc: eng.tensor_scalar(tfc[:, 0:4], tfc[:, 0:4], gfv[:, 3:4], gfv[:, 2:3],
                                                               ALU.mult, ALU.add),
                   reads=[tfc_b, gfv_b], writes=[tfc_b])
            for tau in range(4):
                for (qa, qc, sgn) in ((0, 2, +1), (1, 3, -1)):
                    ta, ba = qb[tau][qa]
                    tc_, bc = qb[tau][qc]
                    s1t, s1b = spares[0]
                    s2t, s2b = spares[1]
                    ca = tfc[:, (tau if sgn > 0 else 4 + tau):(tau if sgn > 0 else 4 + tau) + 1]
                    cb = tfc[:, (4 + tau if sgn > 0 else tau):(4 + tau if sgn > 0 else tau) + 1]
                    sch.op("v", lambda eng, o=s1t, i0=tc_, sc=ca, i1=ta:
                           eng.scalar_tensor_tensor(o[:], i0[:], sc, i1[:], ALU.mult, ALU.add),
                           reads=[bc, ba, tfc_b], writes=[s1b])
                    sch.op("v", lambda eng, o=s2t, i0=ta, sc=cb, i1=tc_:
                           eng.scalar_tensor_tensor(o[:], i0[:], sc, i1[:], ALU.mult, ALU.add),
                           reads=[ba, bc, tfc_b], writes=[s2b])
                    qb[tau][qa] = spares[0]
                    qb[tau][qc] = spares[1]
                    spares = [(ta, ba), (tc_, bc)]
        if b + 1 < NB:
            stage_block(b + 1)
        for tau in range(4):
            y_t, y_buf = yf[tau]
            st_t, st_buf = st4[tau % 2]
            for qt in range(4):
                t, bfq = qb[tau][qt]
                sch.op("v", lambda eng, o=y_t, qt=qt, t=t, x=x1l_t, tau=tau:
                       eng.tensor_tensor(o[:, 256 * qt:256 * (qt + 1)], t[:], x[:, tau, 256 * qt:256 * (qt + 1)],
                                         ALU.add),
                       reads=[bfq, x1l_buf], writes=[y_buf])
            sch.op("v", lambda eng, i=y_t, st=st_t: eng.bn_stats(st[:, 0:6], i[:, 0:512]),
                   reads=[y_buf], writes=[st_buf])
            sch.op("v", lambda eng, i=y_t, st=st_t: eng.bn_stats(st[:, 6:12], i[:, 512:1024]),
                   reads=[y_buf], writes=[st_buf])
            sch.op("v", lambda eng, st=st_t, tau=tau: eng.bn_aggr(fin[:, 2 * tau:2 * tau + 2], st[:, 0:12]),
                   reads=[st_buf], writes=[fin_b])
        sch.op("v", lambda eng: eng.tensor_scalar(
            fin[:, 8:12], fin[:].rearrange("p (t k) -> p k t", k=2)[:, 0, 0:4], -1.0, None, ALU.mult),
            reads=[fin_b], writes=[fin_b])
        sch.op("s", lambda eng: eng.activation(
            fin[:, 12:16], fin[:].rearrange("p (t k) -> p k t", k=2)[:, 1, 0:4], AF.Sqrt, bias=epsc[:]),
            reads=[fin_b, epsc_b], writes=[fin_b])
        sch.op("v", lambda eng: eng.reciprocal(fin[:, 16:20], fin[:, 12:16]),
               reads=[fin_b], writes=[fin_b])
        for tau in range(4):
            g = 4 * b + tau
            y_t, y_buf = yf[tau]
            o_t, o_buf = ot[tau % 2]
            sch.op("v", lambda eng, o=o_t, i=y_t, tau=tau:
                   eng.tensor_scalar(o[:], i[:], fin[:, 8 + tau:9 + tau], fin[:, 16 + tau:17 + tau],
                                     ALU.add, ALU.mult),
                   reads=[y_buf, fin_b], writes=[o_buf])
            load("q", out_d[128 * g:128 * (g + 1), :], o_t[:], [o_buf], [out_b])

    sch.drain_dmas()
    sch.emit_block()
    p4.close()
    p4w.close()
    root.close()
    return nc


def _host_constants():
    pos = np.arange(S, dtype=np.float64)
    cos = np.ones((128, S), dtype=np.float64)
    sin = np.zeros((128, S), dtype=np.float64)
    for p in range(128):
        dd = p % 64
        if dd < ROPE:
            j = dd // 2
            invf = 10000.0 ** (-(2.0 * j) / ROPE)
            ang = pos * invf
            cos[p] = np.cos(ang)
            sin[p] = np.sin(ang)
    r128 = np.zeros((128, 128), dtype=np.float32)
    for blk in range(2):
        o = 64 * blk
        for j in range(ROPE // 2):
            r128[o + 2 * j + 1, o + 2 * j] = -1.0
            r128[o + 2 * j, o + 2 * j + 1] = 1.0
    tri = (np.arange(128)[:, None] <= np.arange(128)[None, :]).astype(np.float32)
    return (cos.astype(np.float32), sin.astype(np.float32),
            r128.astype(BF), tri.astype(BF))


_NC_CACHE = [None]


def kernel(L0_attn_hidden_states, L0_attn_w_qkv, L0_attn_w_out,
           L0_ffn_cn_w1, L0_ffn_cn_b1, L0_ffn_cn_w2, L0_ffn_cn_b2,
           L0_ffn_cn_w3, L0_ffn_cn_b3, L0_ffn_cn_w4, L0_ffn_cn_b4,
           L0_ffn_gc_w1, L0_ffn_gc_b1, L0_ffn_gc_w2, L0_ffn_gc_b2,
           L0_ffn_gc_w3, L0_ffn_gc_b3, L0_ffn_adaptive_range):
    x = np.ascontiguousarray(np.asarray(L0_attn_hidden_states, dtype=np.float32))
    w_qkv = np.asarray(L0_attn_w_qkv, dtype=np.float32)
    w_out = np.asarray(L0_attn_w_out, dtype=np.float32)

    perm = np.empty(3 * D, dtype=np.int64)
    for c in range(3 * D):
        kind, r = divmod(c, D)
        h, dd = divmod(r, HD)
        base = (h // 4) * 768 + (h % 4) * 64 + dd
        perm[c] = base + (0 if kind == 0 else (512 if kind == 1 else 256))
    wqkvT = np.ascontiguousarray(w_qkv[perm, :].T)
    woutT = np.ascontiguousarray(w_out.T)

    rep = lambda a: np.ascontiguousarray(np.repeat(np.asarray(a, np.float32).reshape(1, -1), 128, axis=0))
    colmaj = lambda a, k: np.ascontiguousarray(np.asarray(a, np.float32).reshape(k, 128).T)

    cos_f, sin_f, r128, tri = _host_constants()
    in_common = dict(
        wqkvT=wqkvT, woutT=woutT,
        cnw1=np.ascontiguousarray(np.asarray(L0_ffn_cn_w1, np.float32)),
        cnw2=np.ascontiguousarray(np.asarray(L0_ffn_cn_w2, np.float32)),
        cnw3=np.ascontiguousarray(np.asarray(L0_ffn_cn_w3, np.float32)),
        cnw4=colmaj(L0_ffn_cn_w4, 2),
        gcw1=np.ascontiguousarray(np.asarray(L0_ffn_gc_w1, np.float32)),
        gcw2=np.ascontiguousarray(np.asarray(L0_ffn_gc_w2, np.float32)),
        gcw3r=np.ascontiguousarray(np.repeat(np.asarray(L0_ffn_gc_w3, np.float32).reshape(256, 1), 128, axis=1)),
        b1=colmaj(L0_ffn_cn_b1, 8), b2=colmaj(L0_ffn_cn_b2, 4), b3=colmaj(L0_ffn_cn_b3, 2),
        b4=rep(L0_ffn_cn_b4).reshape(128, 1),
        g1=colmaj(L0_ffn_gc_b1, 4), g2=colmaj(L0_ffn_gc_b2, 2),
        g3=rep(L0_ffn_gc_b3).reshape(128, 1),
        ar=rep(L0_ffn_adaptive_range).reshape(128, 1),
        identb=np.eye(128, dtype=BF), identf=np.eye(128, dtype=np.float32),
        tri=tri, onescol=np.ones((128, 1), dtype=BF),
        cosf=cos_f, sinf=sin_f, r128t=r128,
    )
    if _NC_CACHE[0] is None:
        _NC_CACHE[0] = build_nc()
    nc = _NC_CACHE[0]
    in_maps = [dict(in_common, x=np.ascontiguousarray(x[i])) for i in range(8)]
    res = run_bass_kernel_spmd(nc, in_maps, core_ids=list(range(8)))
    return np.stack([np.asarray(r["out"], dtype=np.float32) for r in res.results], axis=0)


if __name__ == "__main__":
    np.random.seed(0)
    import reference as _ref
    inputs = _ref.setup_inputs(0)
    got = kernel(**{k: np.asarray(v) for k, v in inputs.items()})
    want = np.asarray(_ref.reference(**inputs))
    num = float(np.mean((got - want) ** 2))
    den = float(np.mean(want ** 2))
    print("resid-var ratio:", num / den)
